# version 05
# baseline (speedup 1.0000x reference)
import numpy as np
from contextlib import ExitStack

import concourse.bass as bass
import concourse.mybir as mybir
from concourse.bass_utils import run_bass_kernel_spmd

F32 = mybir.dt.float32
BF16 = mybir.dt.bfloat16
AF = mybir.ActivationFunctionType
ALU = mybir.AluOpType
AX = mybir.AxisListType.X

G = 8
TT = 512
WM_TILES, MAIN_TILES = 6, 8
NT = WM_TILES + MAIN_TILES
TOK_W, TOK_M = WM_TILES * TT, MAIN_TILES * TT
TOK_E = TOK_W + TOK_M
D = 1024
HFG = 32
HMG = 16
NEX = 9
NCORES = 8
LNEPS = 1e-5
NBLK = 5 * G * G + HFG * G + G * HFG + NEX * HMG * G * 2 + NEX * G * HMG

DBG = []
PC_STS = 4
PC_PARTS = 3
PC_NEX = NEX


def _blk(a, KG, MG):
    K, M = a.shape
    assert K == KG * 128 and M == MG * 128
    return np.ascontiguousarray(a.reshape(KG, 128, MG, 128).transpose(2, 0, 1, 3))


def host_prep(inputs):
    f = np.float32
    x = np.asarray(inputs["L0_attn_x"], f)
    blocks = []
    add = lambda arr: blocks.append(arr.reshape(-1, 128, 128))
    add(_blk(np.ascontiguousarray(np.asarray(inputs["L0_attn_att_Wk"], f).T), G, G))
    add(_blk(np.ascontiguousarray(np.asarray(inputs["L0_attn_att_Wv"], f).T), G, G))
    add(_blk(np.ascontiguousarray(np.asarray(inputs["L0_attn_att_Wr"], f).T), G, G))
    add(_blk(np.ascontiguousarray(np.asarray(inputs["L0_attn_att_Wo"], f).T), G, G))
    add(_blk(np.ascontiguousarray(np.asarray(inputs["L0_attn_ffn_Wr"], f).T), G, G))
    add(_blk(np.ascontiguousarray(np.asarray(inputs["L0_attn_ffn_Wk"], f).T), G, HFG))
    add(_blk(np.ascontiguousarray(np.asarray(inputs["L0_attn_ffn_Wv"], f).T), HFG, G))
    w1 = np.concatenate([np.asarray(inputs["L0_ffn_w1"], f),
                         np.asarray(inputs["L0_ffn_sw1"], f)], 0)
    w3 = np.concatenate([np.asarray(inputs["L0_ffn_w3"], f),
                         np.asarray(inputs["L0_ffn_sw3"], f)], 0)
    w2 = np.concatenate([np.asarray(inputs["L0_ffn_w2"], f),
                         np.asarray(inputs["L0_ffn_sw2"], f)], 0)
    add(np.stack([_blk(w1[e], G, HMG) for e in range(NEX)]))
    add(np.stack([_blk(w3[e], G, HMG) for e in range(NEX)]))
    add(np.stack([_blk(w2[e], HMG, G) for e in range(NEX)]))
    wall = np.ascontiguousarray(np.concatenate(blocks, 0)).reshape(NBLK, 128 * 128)

    names = ["time_decay", "time_first", "time_mix_k", "time_mix_v",
             "time_mix_r", "cm_k", "cm_r", "ln1_g", "ln1_b", "ln2_g", "ln2_b"]
    chp = np.ascontiguousarray(np.concatenate(
        [np.asarray(inputs["L0_attn_" + n], f).reshape(G, 128).T for n in names], 1))
    rwtb = np.ascontiguousarray(
        np.asarray(inputs["L0_ffn_router_w"], f).T.reshape(G, 128, 8))
    ebias = np.asarray(inputs["L0_ffn_experts_bias"], f).reshape(8, 1)
    ident = np.eye(128, dtype=f)
    onesc = np.ones((128, 1), f)
    onesr = np.ones((1, 128), f)
    in_maps = []
    for c in range(NCORES):
        b, s = c // 4, c % 4
        t0 = s * TOK_M
        if s == 0:
            xcore = np.concatenate([np.zeros((TOK_W, D), f), x[b, 0:TOK_M]], 0)
            hm = np.zeros((128, 1), f)
        else:
            xcore = np.ascontiguousarray(x[b, t0 - TOK_W: t0 + TOK_M])
            hm = np.ones((128, 1), f)
        in_maps.append(dict(xe=xcore, wall=wall, chp=chp, rwtb=rwtb, ebias=ebias,
                            ident=ident, onesc=onesc, onesr=onesr, hmask=hm))
    return in_maps


class MK:
    def __init__(self):
        nc = bass.Bass("TRN2", target_bir_lowering=False)
        self.nc = nc
        self.stream = {e: [] for e in ("pe", "dv", "ac", "pl", "sy")}
        self.cnt = {}
        self.S = {}
        self.semnames = []
        self.war = {}
        self.ring_user = {}
        self.declare_dram()
        self.layout_heap()

    def sem(self, n):
        if n not in self.cnt:
            self.cnt[n] = 0
            self.semnames.append(n)
        return n

    def bump(self, n, k=1):
        self.cnt[n] = self.cnt.get(n, 0) + k
        return self.cnt[n]

    def E(self, eng, fn, waits=(), inc=None, inc_k=1):
        if inc:
            self.bump(inc, inc_k)
        S = self.S
        ws = tuple((s, v) for (s, v) in waits if v and v > 0)

        def closure(e, fn=fn, ws=ws, inc=inc, inc_k=inc_k):
            for s, v in ws:
                e.wait_ge(S[s], v)
            ins = fn(e)
            if inc is not None and ins is not None:
                ins.then_inc(S[inc], inc_k)
        self.stream[eng].append(closure)
        return (inc, self.cnt[inc]) if inc else None

    def drain(self, eng):
        self.stream[eng].append(lambda e: e.drain())

    def ms(self, eng):
        n = {"pe": "PE", "dv": "DV", "ac": "AC", "pl": "PL"}[eng]
        return (n, self.cnt.get(n, 0))

    def declare_dram(self):
        nc = self.nc
        self.xe = nc.dram_tensor("xe", [TOK_E, D], F32, kind="ExternalInput").ap()
        self.wall = nc.dram_tensor("wall", [NBLK, 128 * 128], F32,
                                   kind="ExternalInput").ap()
        self.chp_d = nc.dram_tensor("chp", [128, 11 * G], F32,
                                    kind="ExternalInput").ap()
        self.rwtb_d = nc.dram_tensor("rwtb", [G, 128, 8], F32,
                                     kind="ExternalInput").ap()
        self.ebias_d = nc.dram_tensor("ebias", [8, 1], F32,
                                      kind="ExternalInput").ap()
        self.ident_d = nc.dram_tensor("ident", [128, 128], F32,
                                      kind="ExternalInput").ap()
        self.onesc_d = nc.dram_tensor("onesc", [128, 1], F32,
                                      kind="ExternalInput").ap()
        self.onesr_d = nc.dram_tensor("onesr", [1, 128], F32,
                                      kind="ExternalInput").ap()
        self.hmask_d = nc.dram_tensor("hmask", [128, 1], F32,
                                      kind="ExternalInput").ap()
        self.y = nc.dram_tensor("y", [TOK_M, D], F32, kind="ExternalOutput").ap()
        ik = lambda nm: "ExternalOutput" if nm in DBG else "Internal"
        self.wbf = nc.dram_tensor("wbf", [NBLK, 128 * 128], BF16,
                                  kind="Internal").ap()
        self.xn_sc = nc.dram_tensor("xn_sc", [D, TOK_M], F32, kind=ik("xn_sc")).ap()
        self.s_sc = nc.dram_tensor("s_sc", [D, TOK_M], F32, kind=ik("s_sc")).ap()
        self.cw_sc = nc.dram_tensor("cw_sc", [NEX - 1, 2 * TT], F32,
                                    kind=ik("cw_sc")).ap()
        self.n0_dbg = (nc.dram_tensor("n0_dbg", [D, TOK_M], F32,
                                      kind="ExternalOutput").ap()
                       if "n0_dbg" in DBG else None)
        self.d0_dbg = (nc.dram_tensor("d0_dbg", [D, TOK_M], F32,
                                      kind="ExternalOutput").ap()
                       if "d0_dbg" in DBG else None)
        self.rw_dbg = (nc.dram_tensor("rw_dbg", [D, TOK_M], F32,
                                      kind="ExternalOutput").ap()
                       if "rw_dbg" in DBG else None)
        self.gk_dbg = (nc.dram_tensor("gk_dbg", [D, TOK_M], F32,
                                      kind="ExternalOutput").ap()
                       if "gk_dbg" in DBG else None)
        self.q_dbg = (nc.dram_tensor("q_dbg", [D, TOK_M], F32,
                                     kind="ExternalOutput").ap()
                      if "q_dbg" in DBG else None)
        self.u_dbg = (nc.dram_tensor("u_dbg", [D, TOK_M], F32,
                                     kind="ExternalOutput").ap()
                      if "u_dbg" in DBG else None)
        self.RG = {}
        rb = 0
        for nm, n in [("att_k", G * G), ("att_v", G * G), ("att_r", G * G),
                      ("wo", G * G), ("fwr", G * G), ("fwk", HFG * G),
                      ("fwv", G * HFG), ("moe1", NEX * HMG * G),
                      ("moe3", NEX * HMG * G), ("moe2", NEX * G * HMG)]:
            self.RG[nm] = rb
            rb += n
        assert rb == NBLK

    def layout_heap(self):
        self.off = {}
        top = [0]

        def al(name, cols):
            self.off[name] = top[0]
            top[0] += cols
        al("chp", 11 * G); al("der", 3 * G); al("ident", 128); al("onesc", 1)
        al("onesr", 128); al("ebias", 1); al("hmask", 1); al("rwt", G * 8)
        al("stats", 2 * TT); al("stats2", 2 * TT); al("small", 64); al("rwhb", 8); al("eps", 1); al("onesb", 1)
        al("lgsb", TT); al("lgT", 64); al("cwt", 64); al("t2", 64)
        al("cwrow", 2 * TT); al("cwb", 2 * TT)
        al("scr", 2 * TT); al("scr2", TT)
        al("p0in", 2 * 1024); al("p0out", 1024)
        al("xtm", 2 * 1024)
        al("ring", 6144)
        al("projbf", 2 * 2048)
        al("rwkvb", 2048)
        al("xmid", G * TT); al("xfm", G * TT); al("xn", G * TT); al("xa", G * (TT + 1))
        al("dA", G * TT); al("qA", G * TT); al("uA", G * TT)
        self.HEAPC = top[0]
        assert self.HEAPC <= 53100, self.HEAPC

    def enter(self, ctx):
        nc = self.nc
        self.heap = ctx.enter_context(
            nc.sbuf_tensor("heap", [128, self.HEAPC], F32)).ap()
        self.heapb = self.heap.bitcast(BF16)
        self.ps = [ctx.enter_context(nc.psum_tensor(f"ps{i}", [128, TT], F32))
                   for i in range(8)]
        self.psap = [p.ap() for p in self.ps]
        for nm in ["PE", "DV", "AC", "PL", "LC", "LX0", "LX1",
                   "P0I0", "P0I1", "P0att", "P0wo", "P0ffn", "P0moe",
                   "SXN", "SRW", "SS", "SCW", "SY",
                   "RXN", "RSS", "RCW0", "RCW1", "RX2"]:
            self.sem(nm)
        for i in range(10):
            self.sem(f"RS{i}")
        for n in self.semnames:
            self.S[n] = ctx.enter_context(nc.semaphore(n))

    def hfx(self, name, c0, n, p0=0, np_=128):
        o = self.off[name] + c0
        return self.heap[p0:p0 + np_, o:o + n]

    def hbx(self, name, c0, n, p0=0, np_=128):
        o = 2 * self.off[name] + c0
        return self.heapb[p0:p0 + np_, o:o + n]

    def ar(self, name, gi, n=TT, c0=0):
        return self.hfx(name, gi * TT + c0, n)

    def xav(self, gi, c0, n):
        return self.hfx("xa", gi * (TT + 1) + c0, n)

    def xap(self, gi, n=TT, c0=0):
        return self.hfx("xa", gi * TT + c0, n)

    def pbf(self, buf, gi, n=TT, c0=0):
        return self.hbx("projbf", buf * 4096 + gi * TT + c0, n)

    def rwb(self, gi, n=TT, c0=0):
        return self.hbx("rwkvb", gi * TT + c0, n)

    def gbuf(self, og, n=TT, c0=0):
        if og < 4:
            return self.hfx("p0in", og * TT + c0, n)
        if og < 6:
            return self.hfx("p0out", (og - 4) * TT + c0, n)
        return self.hfx("cwb", (og - 6) * TT + c0, n)

    def kkb(self, hc, n=TT, c0=0):
        reg = "xfm" if hc < 16 else "dA"
        return self.hbx(reg, (hc % 16) * TT + c0, n)

    def chpc(self, j, gi):
        CHI = dict(decay=0, tf=1, tmk=2, tmv=3, tmr=4, cmk=5, cmr=6,
                   g1=7, b1=8, g2=9, b2=10)
        return self.hfx("chp", CHI[j] * G + gi, 1)

    def chps(self, j):
        CHI = dict(decay=0, tf=1, tmk=2, tmv=3, tmr=4, cmk=5, cmr=6,
                   g1=7, b1=8, g2=9, b2=10)
        return self.hfx("chp", CHI[j] * G, G)

    def derc(self, j, gi):
        return self.hfx("der", dict(w=0, alpha=1, etf=2)[j] * G + gi, 1)

    def derg(self, j):
        return self.hfx("der", dict(w=0, alpha=1, etf=2)[j] * G, G)

    def small(self, c0, n=1):
        return self.hfx("small", c0, n)


    def ar3(self, name, c0=0, n=TT):
        a = self.hfx(name, 0, G * TT)
        return a.rearrange("p (g t) -> p g t", g=G)[:, :, c0:c0 + n]

    def xa3(self, c0, n):
        a = self.hfx("xa", 0, G * (TT + 1))
        return a.rearrange("p (g t) -> p g t", g=G, t=TT + 1)[:, :, c0:c0 + n]

    def bc_row(self, ap2, n=TT):
        return ap2.unsqueeze(1).broadcast_to([128, G, n])

    def bc_grp(self, ap2, n=TT):
        return ap2.unsqueeze(2).broadcast_to([128, G, n])

    def pb3(self, buf, c0=0, n=TT):
        b0 = 2 * self.off["projbf"] + buf * 4096
        a = self.heapb[:, b0:b0 + 4096]
        return a.rearrange("p (g t) -> p g t", g=G)[:, :, c0:c0 + n]

    def rw3(self, c0=0, n=TT):
        b0 = 2 * self.off["rwkvb"]
        a = self.heapb[:, b0:b0 + 4096]
        return a.rearrange("p (g t) -> p g t", g=G)[:, :, c0:c0 + n]

    def slot_blk(self, si, bi):
        if si < 8:
            c0 = 2 * (self.off["ring"] + si * 512) + bi * 128
        else:
            c0 = 2 * (self.off["ring"] + 4096 + (si - 8) * 1024) + bi * 128
        return self.heapb[:, c0:c0 + 128]

    def slot_ap(self, si, nblk):
        if si < 8:
            c0 = 2 * (self.off["ring"] + si * 512)
        else:
            c0 = 2 * (self.off["ring"] + 4096 + (si - 8) * 1024)
        return self.heapb[:, c0:c0 + nblk * 128]

    def ring_load(self, si, row0, nblk, extra=()):
        sname = self.sem(f"RS{si}")
        waits = list(extra)
        if si in self.ring_user:
            waits.append(self.ring_user.pop(si))
        self.E("sy", lambda e, a=self.wbf[row0:row0 + nblk, :]
               .rearrange("b (p q) -> p b q", p=128),
               s=self.slot_ap(si, nblk).rearrange("p (b q) -> p b q", q=128):
               e.dma_start(out=s, in_=a),
               waits=waits, inc=sname, inc_k=16)
        return (sname, self.cnt[sname])

    def gen_p0(self):
        self.P0_TOT = {}
        self.p0_jobs = []
        sizes = {"att_k": G * G, "att_v": G * G, "att_r": G * G,
                 "wo": G * G, "fwr": G * G, "fwk": HFG * G, "fwv": G * HFG,
                 "moe1": NEX * HMG * G, "moe3": NEX * HMG * G,
                 "moe2": NEX * G * HMG}
        for nm, rsem, splits in [("att_k", "P0att", 1), ("att_v", "P0att", 1),
                                 ("att_r", "P0att", 1), ("wo", "P0wo", 1),
                                 ("fwr", "P0wo", 1), ("fwk", "P0ffn", 1),
                                 ("fwv", "P0ffn", 1), ("moe1", "P0moe", 4),
                                 ("moe3", "P0moe", 4), ("moe2", "P0moe", 4)]:
            r0 = self.RG[nm]
            per = sizes[nm] // splits
            for sidx in range(splits):
                self.p0_jobs.append((r0 + sidx * per, per, rsem))
                self.P0_TOT[rsem] = self.P0_TOT.get(rsem, 0) + 1

    def emit_p0(self):
        for row0, nrows, rsem in self.p0_jobs:
            self.E("pl", lambda e, o=self.wbf[row0:row0 + nrows, :],
                   i=self.wall[row0:row0 + nrows, :]:
                   e.dma_start(out=o, in_=i), inc=rsem, inc_k=16)

    def pump_p0(self, n):
        pass

    def p0_done(self, rsem):
        return (rsem, 16 * self.P0_TOT[rsem])

    def gen_startup(self):
        for src, dst in [
                (self.chp_d, self.hfx("chp", 0, 11 * G)),
                (self.ident_d, self.hfx("ident", 0, 128)),
                (self.onesc_d, self.hfx("onesc", 0, 1)),
                (self.onesr_d, self.hfx("onesr", 0, 128, 0, 1)),
                (self.ebias_d, self.hfx("ebias", 0, 1, 0, 8)),
                (self.hmask_d, self.hfx("hmask", 0, 1)),
                (self.rwtb_d.rearrange("k p c -> p k c"),
                 self.hfx("rwt", 0, G * 8).rearrange("p (k c) -> p k c", k=G))]:
            self.E("sy", lambda e, s=src, d=dst: e.dma_start(out=d, in_=s),
                   inc="LC", inc_k=16)
        self.LCv = self.cnt["LC"]
        LC = ("LC", self.LCv)
        self.E("ac", lambda e: e.activation(out=self.derg("w"),
                                            in_=self.hfx("chp", 0, G),
                                            func=AF.Exp), waits=[LC], inc="AC")
        self.drain("ac")
        self.E("dv", lambda e: e.tensor_scalar_mul(out=self.derg("w"),
                                                   in0=self.derg("w"),
                                                   scalar1=-1.0),
               waits=[self.ms("ac")], inc="DV")
        self.drain("dv")
        self.E("ac", lambda e: e.activation(out=self.derg("alpha"),
                                            in_=self.derg("w"), func=AF.Exp),
               waits=[self.ms("dv")], inc="AC")
        self.E("ac", lambda e: e.activation(out=self.derg("etf"),
                                            in_=self.hfx("chp", G, G),
                                            func=AF.Exp), inc="AC")
        self.drain("ac")
        self.DERdone = self.ms("ac")
        self.E("dv", lambda e: e.memset(self.hfx("small", 0, 64), 0.0),
               waits=[LC], inc="DV")
        self.E("dv", lambda e: e.memset(self.hfx("eps", 0, 1), LNEPS), inc="DV")
        self.E("dv", lambda e: e.tensor_copy(out=self.hbx("onesb", 0, 1),
                                             in_=self.hfx("onesc", 0, 1)), inc="DV")
        self.drain("dv")

    def ln_stats(self, src3, bfc, sqc, rows, bs, src_ready, sq_war=None):
        G_ = G
        m_row, rs_row, t1, t2 = rows
        w0 = [src_ready] + ([sq_war] if sq_war else [])
        bfc3 = bfc.rearrange("p (g t) -> p g t", g=G_)
        sqc3 = sqc.rearrange("p (g t) -> p g t", g=G_)
        self.E("ac", lambda e: e.copy(out=bfc3, in_=src3), waits=w0)
        self.E("ac", lambda e: e.activation(out=sqc3, in_=src3, func=AF.Square),
               inc="AC")
        acv = self.ms("ac")
        onesb = self.hbx("onesb", 0, 1)
        w1 = [acv, self.war.pop(f"ps{bs}", (None, 0)),
              self.war.pop(f"ps{bs + 1}", (None, 0))]
        for gi in range(G_):
            self.E("pe", lambda e, r=bfc.rearrange("p (g t) -> p g t", g=G_)[:, gi, :],
                   st=(gi == 0), sp=(gi == G_ - 1):
                   e.matmul(self.psap[bs][0:1, :], onesb, r, start=st, stop=sp),
                   waits=w1 if gi == 0 else [])
        for gi in range(G_):
            self.E("pe", lambda e, r=sqc.rearrange("p (g t) -> p g t", g=G_)[:, gi, :],
                   st=(gi == 0), sp=(gi == G_ - 1):
                   e.matmul(self.psap[bs + 1][0:1, :], onesb, r, start=st, stop=sp),
                   inc="PE" if gi == G_ - 1 else None)
        pev = self.ms("pe")
        self.E("dv", lambda e: e.tensor_scalar_mul(out=m_row,
                                                   in0=self.psap[bs][0:1, :],
                                                   scalar1=1.0 / D),
               waits=[pev])
        self.drain("dv")
        self.E("dv", lambda e: e.tensor_mul(out=t1, in0=m_row, in1=m_row))
        self.drain("dv")
        self.E("dv", lambda e: e.scalar_tensor_tensor(
            out=t2, in0=self.psap[bs + 1][0:1, :], scalar=1.0 / D, in1=t1,
            op0=ALU.mult, op1=ALU.subtract), inc="DV")
        self.drain("dv")
        self.E("ac", lambda e: e.activation(out=t1, in_=t2, func=AF.Ln,
                                            bias=self.hfx("eps", 0, 1, 0, 1)),
               waits=[self.ms("dv")])
        self.drain("ac")
        self.E("ac", lambda e: e.activation(out=rs_row, in_=t1, func=AF.Exp,
                                            scale=-0.5), inc="AC")
        self.drain("ac")
        acv2 = self.ms("ac")
        self.E("pe", lambda e: e.matmul(self.psap[bs], self.hfx("onesr", 0, 128, 0, 1),
                                        m_row, start=True, stop=True), waits=[acv2])
        self.E("pe", lambda e: e.matmul(self.psap[bs + 1],
                                        self.hfx("onesr", 0, 128, 0, 1),
                                        rs_row, start=True, stop=True), inc="PE")
        return self.ms("pe")

    def rows_pa(self):
        return (self.hfx("stats", 0, TT, 0, 1), self.hfx("stats", TT, TT, 0, 1),
                self.hfx("stats", 0, TT, 32, 1), self.hfx("stats", TT, TT, 32, 1))

    def rows_pb(self):
        return (self.hfx("stats2", 0, TT, 0, 1), self.hfx("stats2", TT, TT, 0, 1),
                self.hfx("stats2", 0, TT, 32, 1), self.hfx("stats2", TT, TT, 32, 1))

    def gbuf(self, og, n=TT, c0=0):
        if og < 4:
            return self.hfx("p0in", og * TT + c0, n)
        if og < 6:
            return self.hfx("p0out", (og - 4) * TT + c0, n)
        return self.hfx("cwb", (og - 6) * TT + c0, n)

    def kkb(self, hc, n=TT, c0=0):
        if hc < 16:
            return self.hbx("uA", hc * TT + c0, n)
        return self.hbx("projbf", 4096 + (hc - 16) * TT + c0, n)

    def gen_pa_head(self, t):
        E, hfx, ps = self.E, self.hfx, self.psap
        xfm_war = self.war.pop("xfm", None)
        for a in range(4):
            half = a % 2
            lx = f"LX{half}"
            w = [self.war.pop(f"xtm{half}", (None, 0))]
            E("sy", lambda e, r0=t * TT + a * 128, d=hfx("xtm", half * 1024, 1024):
              e.dma_start(out=d, in_=self.xe[r0:r0 + 128, :]),
              waits=w, inc=lx, inc_k=16)
            ld = (lx, self.cnt[lx])
            for gi in range(G):
                bank = gi // 4
                w = []
                if gi == 0:
                    w = [ld, self.war.pop("ps0", (None, 0)),
                         self.war.pop("ps1", (None, 0))]
                    if a == 0 and t == 0:
                        w.append(("LC", self.LCv))
                E("pe", lambda e, i=hfx("xtm", half * 1024 + gi * 128, 128),
                  o=ps[bank][:, (gi % 4) * 128:(gi % 4) * 128 + 128]:
                  e.transpose(o, i, self.hfx("ident", 0, 128)),
                  waits=w, inc="PE" if gi == G - 1 else None)
            pev = self.ms("pe")
            self.war[f"xtm{half}"] = pev
            for bi in range(2):
                w = [pev] if bi == 0 else []
                if bi == 0 and a == 0 and xfm_war:
                    w.append(xfm_war)
                out3 = hfx("xfm", 0, G * TT).rearrange(
                    "p (g t) -> p g t", g=G)[:, 4 * bi:4 * bi + 4, a * 128:a * 128 + 128]
                E("dv", lambda e, i=ps[bi][:, 0:512].rearrange("p (g t) -> p g t", g=4),
                  o=out3: e.tensor_copy(out=o, in_=i),
                  waits=w, inc="DV" if bi == 1 else None)
            dvv = self.ms("dv")
            self.war["ps0"] = dvv
            self.war["ps1"] = dvv
        xfm_ms = self.ms("dv")
        bc_outer = self.ln_stats(self.ar3("xfm"), self.hbx("dA", 0, 4096),
                                 self.hbx("dA", 4096, 4096), self.rows_pa(), 4,
                                 xfm_ms, sq_war=self.war.pop("dA", None))
        is_w = t < WM_TILES
        last_w = t == WM_TILES - 1
        mt = t - WM_TILES
        ar = self.ar

        def xap3(c0, n):
            a = hfx("xa", 0, G * TT)
            return a.rearrange("p (g t) -> p g t", g=G)[:, :, c0:c0 + n]

        sm8 = lambda c0: hfx("small", c0, G).unsqueeze(2)
        xn_war = self.war.pop("xn", None)
        E("dv", lambda e: e.tensor_tensor(
            out=self.ar3("xn"), in0=self.ar3("xfm"),
            in1=self.bc_row(ps[4][:, 0:TT]), op=ALU.subtract),
          waits=[bc_outer] + ([xn_war] if xn_war else []))
        E("dv", lambda e: e.tensor_tensor(
            out=self.ar3("xn"), in0=self.ar3("xn"),
            in1=self.bc_row(ps[5][:, 0:TT]), op=ALU.mult), inc="DV")
        xn_ms = self.ms("dv")
        self.war["ps4"] = xn_ms
        self.war["ps5"] = xn_ms

        if not is_w:
            out_ap = self.xn_sc.rearrange("(g p) t -> p g t", p=128)[
                :, :, mt * TT:(mt + 1) * TT]
            E("sy", lambda e, o=out_ap,
              i=hfx("xn", 0, G * TT).rearrange("p (g t) -> p g t", g=G):
              e.dma_start(out=o, in_=i), waits=[xn_ms], inc="SXN", inc_k=16)
            xn_spill = ("SXN", self.cnt["SXN"])
        else:
            xn_spill = None

        bc = self.ln_stats(self.ar3("xn"), self.hbx("dA", 0, 4096),
                           self.hbx("dA", 4096, 4096), self.rows_pa(), 4,
                           xn_ms, sq_war=None)
        xa_war = self.war.pop("xa", None)
        if t == 0:
            E("dv", lambda e: e.memset(self.xa3(0, 1), 0.0))
        else:
            E("dv", lambda e: e.tensor_copy(out=self.xa3(0, 1), in_=sm8(0)),
              waits=[xa_war] if xa_war else [])
        if t == WM_TILES:
            self.drain("dv")
            E("dv", lambda e: e.tensor_scalar_mul(
                out=self.xa3(0, 1), in0=self.xa3(0, 1),
                scalar1=self.hfx("hmask", 0, 1)))
            E("dv", lambda e: e.tensor_scalar_mul(
                out=hfx("small", 16, G), in0=hfx("small", 16, G),
                scalar1=self.hfx("hmask", 0, 1)))
            E("dv", lambda e: e.tensor_scalar_mul(
                out=hfx("small", 24, G), in0=hfx("small", 24, G),
                scalar1=self.hfx("hmask", 0, 1)))
        self.drain("dv")
        E("dv", lambda e: e.tensor_tensor(
            out=self.xa3(1, TT), in0=self.ar3("xn"),
            in1=self.bc_row(ps[4][:, 0:TT]), op=ALU.subtract), waits=[bc])
        E("dv", lambda e: e.tensor_tensor(
            out=self.xa3(1, TT), in0=self.xa3(1, TT),
            in1=self.bc_row(ps[5][:, 0:TT]), op=ALU.mult))
        E("dv", lambda e: e.tensor_tensor(
            out=self.xa3(1, TT), in0=self.xa3(1, TT),
            in1=self.bc_grp(self.chps("g1")), op=ALU.mult))
        E("dv", lambda e: e.tensor_tensor(
            out=self.xa3(1, TT), in0=self.xa3(1, TT),
            in1=self.bc_grp(self.chps("b1")), op=ALU.add), inc="DV")
        xa_ms = self.ms("dv")
        self.war["ps4"] = xa_ms
        self.war["ps5"] = xa_ms
        E("dv", lambda e: e.tensor_copy(out=sm8(0), in_=self.xa3(TT, 1)))
        self.drain("dv")

        return (xn_spill, bc)

    def gen_pa_tail(self, t, head_out):
        is_w = t < WM_TILES
        last_w = t == WM_TILES - 1
        mt = t - WM_TILES
        xn_spill, bc = head_out
        E, ar, hfx, ps = self.E, self.ar, self.hfx, self.psap

        def xap3(c0, n):
            a = hfx("xa", 0, G * TT)
            return a.rearrange("p (g t) -> p g t", g=G)[:, :, c0:c0 + n]

        sm8 = lambda c0: hfx("small", c0, G).unsqueeze(2)

        pj_war = self.war.pop("projbf", None)
        E("dv", lambda e: e.tensor_tensor(
            out=self.ar3("dA"), in0=self.xa3(1, TT), in1=self.xa3(0, TT),
            op=ALU.subtract), waits=[bc])
        self.drain("dv")
        for gi in range(G):
            w = [pj_war] if (gi == 0 and pj_war) else []
            E("dv", lambda e, d_=ar("dA", gi), x0=self.xav(gi, 0, TT),
              o=self.pbf(0, gi), s1=self.chpc("tmk", gi):
              e.scalar_tensor_tensor(out=o, in0=d_, scalar=s1, in1=x0,
                                     op0=ALU.mult, op1=ALU.add), waits=w)
        pj1_war = self.war.pop("projbf1", None)
        for gi in range(G):
            w = [pj1_war] if (gi == 0 and pj1_war) else []
            E("dv", lambda e, d_=ar("dA", gi), x0=self.xav(gi, 0, TT),
              o=self.pbf(1, gi), s1=self.chpc("tmv", gi):
              e.scalar_tensor_tensor(out=o, in0=d_, scalar=s1, in1=x0,
                                     op0=ALU.mult, op1=ALU.add),
              waits=w, inc="DV" if gi == G - 1 else None)
        if (not is_w) or last_w:
            rw_war = self.war.pop("rwkvb", None)
            for gi in range(G):
                w = [rw_war] if (gi == 0 and rw_war) else []
                E("dv", lambda e, d_=ar("dA", gi), x0=self.xav(gi, 0, TT),
                  o=self.rwb(gi), s1=self.chpc("tmr", gi):
                  e.scalar_tensor_tensor(out=o, in0=d_, scalar=s1, in1=x0,
                                         op0=ALU.mult, op1=ALU.add),
                  waits=w, inc="DV" if gi == G - 1 else None)
        mix_ms = self.ms("dv")


        q_war = self.war.pop("qA", None)
        u_war = self.war.pop("uA", None)
        qrd = {}
        for proj in range(2):
            reg = "att_k" if proj == 0 else "att_v"
            for og in range(G):
                lw = self.ring_load(og, self.RG[reg] + og * G, G,
                                    extra=([self.p0_done("P0att")]
                                           if t == 0 and proj == 0 else []))
                bank = 2 + og % 2
                w0 = [lw, mix_ms, self.war.pop(f"ps{bank}", (None, 0))]
                for kg in range(G):
                    E("pe", lambda e, l=self.slot_blk(og, kg), r=self.pbf(proj, kg),
                      b=bank, st=(kg == 0), sp=(kg == G - 1):
                      e.matmul(ps[b], l, r, start=st, stop=sp),
                      waits=w0 if kg == 0 else [],
                      inc="PE" if kg == G - 1 else None)
                self.ring_user[og] = self.ms("pe")
                if proj == 0:
                    w = [self.ms("pe")]
                    if og == 0 and q_war:
                        w.append(q_war)
                    E("ac", lambda e, b=bank, o=ar("qA", og):
                      e.activation(out=o, in_=ps[b], func=AF.Exp),
                      waits=w, inc="AC")
                    self.war[f"ps{bank}"] = self.ms("ac")
                    qrd[og] = self.ms("ac")
                else:
                    w = [self.ms("pe"), qrd[og]]
                    if og == 0 and u_war:
                        w.append(u_war)
                    E("dv", lambda e, b=bank, q_=ar("qA", og), o=ar("uA", og):
                      e.tensor_tensor(out=o, in0=q_, in1=ps[b], op=ALU.mult),
                      waits=w, inc="DV")
                    self.war[f"ps{bank}"] = self.ms("dv")
        qu_ms = self.ms("dv")

        for gi in range(G):
            w = [qu_ms] if gi == 0 else []
            if t == 0 and gi == 0:
                w.append(self.DERdone)
            ini_n = 0.0 if t == 0 else hfx("small", 16 + gi, 1)
            ini_d = 0.0 if t == 0 else hfx("small", 24 + gi, 1)
            E("dv", lambda e, al_=self.derc("alpha", gi).broadcast_to([128, TT]),
              u_=ar("uA", gi), o=ar("xfm", gi), ini=ini_n:
              e.tensor_tensor_scan(out=o, data0=al_, data1=u_, initial=ini,
                                   op0=ALU.mult, op1=ALU.add), waits=w)
            E("dv", lambda e, al_=self.derc("alpha", gi).broadcast_to([128, TT]),
              q_=ar("qA", gi), o=self.xap(gi), ini=ini_d:
              e.tensor_tensor_scan(out=o, data0=al_, data1=q_, initial=ini,
                                   op0=ALU.mult, op1=ALU.add),
              inc="DV" if gi == G - 1 else None)
        self.drain("dv")

        etfb = self.bc_grp(self.derg("etf"), TT - 1)
        etfb1 = self.bc_grp(self.derg("etf"), 1)
        E("dv", lambda e: e.tensor_tensor(
            out=self.ar3("dA", 1, TT - 1), in0=self.ar3("uA", 1, TT - 1),
            in1=etfb, op=ALU.mult))
        E("dv", lambda e: e.tensor_tensor(
            out=self.ar3("dA", 1, TT - 1), in0=self.ar3("dA", 1, TT - 1),
            in1=self.ar3("xfm", 0, TT - 1), op=ALU.add))
        E("dv", lambda e: e.tensor_tensor(
            out=self.ar3("dA", 0, 1), in0=self.ar3("uA", 0, 1),
            in1=etfb1, op=ALU.mult))
        self.drain("dv")
        if t > 0:
            E("dv", lambda e: e.tensor_tensor(
                out=self.ar3("dA", 0, 1), in0=self.ar3("dA", 0, 1),
                in1=sm8(16), op=ALU.add))
            self.drain("dv")
        E("dv", lambda e: e.tensor_tensor(
            out=self.ar3("uA", 1, TT - 1), in0=self.ar3("qA", 1, TT - 1),
            in1=etfb, op=ALU.mult))
        E("dv", lambda e: e.tensor_tensor(
            out=self.ar3("uA", 1, TT - 1), in0=self.ar3("uA", 1, TT - 1),
            in1=xap3(0, TT - 1), op=ALU.add))
        E("dv", lambda e: e.tensor_tensor(
            out=self.ar3("uA", 0, 1), in0=self.ar3("qA", 0, 1),
            in1=etfb1, op=ALU.mult))
        self.drain("dv")
        if t > 0:
            E("dv", lambda e: e.tensor_tensor(
                out=self.ar3("uA", 0, 1), in0=self.ar3("uA", 0, 1),
                in1=sm8(24), op=ALU.add))
            self.drain("dv")
        E("dv", lambda e: e.tensor_copy(out=sm8(16), in_=self.ar3("xfm", TT - 1, 1)))
        E("dv", lambda e: e.tensor_copy(out=sm8(24), in_=xap3(TT - 1, 1)), inc="DV")
        self.drain("dv")
        n0_ms = self.ms("dv")

        if is_w and not last_w:
            for k in ["dA", "qA", "uA", "xfm", "xn"]:
                self.war[k] = n0_ms
            self.war["projbf"] = self.ms("pe")
            self.war["projbf1"] = self.ms("pe")
            self.pump_p0(88)
            return None

        for og in range(G):
            lw = self.ring_load(og, self.RG["att_r"] + og * G, G)
            bank = 2 + og % 2
            w0 = [lw, self.war.pop(f"ps{bank}", (None, 0))]
            if og == 0:
                w0.append(n0_ms)
            if last_w:
                for kg in range(G):
                    E("pe", lambda e, l=self.slot_blk(og, kg),
                      r=self.rwb(kg, 1, TT - 1), b=bank,
                      st=(kg == 0), sp=(kg == G - 1):
                      e.matmul(ps[b][:, 0:1], l, r, start=st, stop=sp),
                      waits=w0 if kg == 0 else [],
                      inc="PE" if kg == G - 1 else None)
                self.ring_user[og] = self.ms("pe")
                E("ac", lambda e, b=bank, o=hfx("small", 48 + og, 1):
                  e.activation(out=o, in_=ps[b][:, 0:1], func=AF.Sigmoid),
                  waits=[self.ms("pe")], inc="AC")
            else:
                for kg in range(G):
                    E("pe", lambda e, l=self.slot_blk(og, kg), r=self.rwb(kg),
                      b=bank, st=(kg == 0), sp=(kg == G - 1):
                      e.matmul(ps[b], l, r, start=st, stop=sp),
                      waits=w0 if kg == 0 else [],
                      inc="PE" if kg == G - 1 else None)
                self.ring_user[og] = self.ms("pe")
                E("ac", lambda e, b=bank, o=self.xap(og):
                  e.activation(out=o, in_=ps[b], func=AF.Sigmoid),
                  waits=[self.ms("pe")], inc="AC")
            self.war[f"ps{bank}"] = self.ms("ac")
        sr_ms = self.ms("ac")

        if last_w:
            for gi in range(G):
                E("dv", lambda e, i=ar("uA", gi, 1, TT - 1), o=hfx("t2", gi, 1):
                  e.reciprocal(out=o, in_=i), waits=[sr_ms] if gi == 0 else [])
            self.drain("dv")
            for gi in range(G):
                E("dv", lambda e, a=ar("dA", gi, 1, TT - 1), b=hfx("t2", gi, 1),
                  o=hfx("small", 32 + gi, 1):
                  e.tensor_tensor(out=o, in0=a, in1=b, op=ALU.mult))
            self.drain("dv")
            for gi in range(G):
                E("dv", lambda e, o=hfx("small", 32 + gi, 1),
                  s=hfx("small", 48 + gi, 1):
                  e.tensor_scalar_mul(out=o, in0=o, scalar1=s))
            self.drain("dv")
            E("dv", lambda e: e.tensor_copy(out=self.hbx("rwhb", 0, 8),
                                            in_=hfx("small", 32, G)))
            self.drain("dv")
            E("dv", lambda e: e.tensor_copy(out=sm8(40),
                                            in_=self.ar3("xn", TT - 1, 1)),
              inc="DV")
            self.drain("dv")
            hal = self.ms("dv")
            for k in ["dA", "qA", "uA", "xfm", "xn", "xa"]:
                self.war[k] = hal
            self.war["projbf"] = self.ms("pe")
            self.war["projbf1"] = self.ms("pe")
            self.pump_p0(88)
            return None

        E("ac", lambda e: e.activation(out=hfx("qA", 0, G * TT),
                                       in_=hfx("uA", 0, G * TT), func=AF.Ln),
          waits=[n0_ms])
        E("ac", lambda e: e.activation(out=hfx("xfm", 0, G * TT),
                                       in_=hfx("qA", 0, G * TT), func=AF.Exp,
                                       scale=-1.0), inc="AC")
        rec_ms = self.ms("ac")
        self.war["qA"] = rec_ms
        self.war["uA"] = rec_ms
        E("dv", lambda e: e.tensor_tensor(out=hfx("dA", 0, G * TT),
                                          in0=hfx("dA", 0, G * TT),
                                          in1=hfx("xfm", 0, G * TT), op=ALU.mult),
          waits=[rec_ms])
        E("dv", lambda e: e.tensor_tensor(out=self.hbx("rwkvb", 0, 4096),
                                          in0=hfx("dA", 0, G * TT),
                                          in1=hfx("xa", 0, G * TT), op=ALU.mult),
          waits=[sr_ms], inc="DV")
        rw_ms = self.ms("dv")
        self.war["dA"] = rw_ms
        self.war["xfm"] = rw_ms
        self.pump_p0(8)
        return (mt, xn_spill, rw_ms)

    def gen_pb_a(self, pend):
        mt, xn_spill, rw_ms = pend
        E, ar, hfx, ps = self.E, self.ar, self.hfx, self.psap
        sm8 = lambda c0: hfx("small", c0, G).unsqueeze(2)

        xmid_war = self.war.pop("xmid", None)
        for cg in range(G):
            lw = self.ring_load(cg, self.RG["wo"] + cg * G, G,
                                extra=([self.p0_done("P0wo")] if mt == 0 else []))
            bank = 6 + cg % 2
            w0 = [lw, self.war.pop(f"ps{bank}", (None, 0))]
            if cg == 0:
                w0.append(rw_ms)
            for ko in range(G):
                E("pe", lambda e, l=self.slot_blk(cg, ko), r=self.rwb(ko),
                  b=bank, st=(ko == 0), sp=(ko == G - 1):
                  e.matmul(ps[b], l, r, start=st, stop=sp),
                  waits=w0 if ko == 0 else [],
                  inc="PE" if ko == G - 1 else None)
            self.ring_user[cg] = self.ms("pe")
            w = [self.ms("pe")]
            if cg == 0 and xmid_war:
                w.append(xmid_war)
            E("dv", lambda e, b=bank, i=ar("xn", cg), o=ar("xmid", cg):
              e.tensor_tensor(out=o, in0=i, in1=ps[b], op=ALU.add),
              waits=w, inc="DV")
            self.war[f"ps{bank}"] = self.ms("dv")
        xmid_ms = self.ms("dv")
        self.war["rwkvb"] = self.ms("pe")
        wo_read_xn = self.ms("dv")
        self.war["xn"] = (xn_spill[0], xn_spill[1]) if xn_spill else wo_read_xn

        if mt == 0:
            w0 = [self.war.pop("ps2", (None, 0)), self.war.pop("ps3", (None, 0)),
                  self.ms("dv")]
            for cg in range(G):
                lw = self.ring_load(cg, self.RG["wo"] + cg * G, G)
                for ko in range(G):
                    E("pe", lambda e, l=self.slot_blk(cg, ko),
                      r=self.hbx("rwhb", ko, 1), b=2, c=cg,
                      st=(ko == 0), sp=(ko == G - 1):
                      e.matmul(ps[b][:, c:c + 1], l, r, start=st, stop=sp),
                      waits=(w0 + [lw]) if ko == 0 else [],
                      inc="PE" if ko == G - 1 else None)
                self.ring_user[cg] = self.ms("pe")
            pev = self.ms("pe")
            t2 = lambda c0, n=G: hfx("t2", c0, n)
            E("dv", lambda e: e.tensor_tensor(out=t2(8), in0=hfx("small", 40, G),
                                              in1=ps[2][:, 0:G], op=ALU.add),
              waits=[pev], inc="DV")
            self.drain("dv")
            xmh_dv = self.ms("dv")
            E("ac", lambda e: e.activation(out=t2(16), in_=t2(8), func=AF.Square),
              waits=[xmh_dv], inc="AC")
            self.drain("ac")
            acv = self.ms("ac")
            for cg in range(G):
                E("pe", lambda e, l=t2(8 + cg, 1), c=cg:
                  e.matmul(ps[3][0:1, c:c + 1], l, self.hfx("onesc", 0, 1),
                           start=True, stop=True),
                  waits=[xmh_dv, acv] if cg == 0 else [])
            for cg in range(G):
                E("pe", lambda e, l=t2(16 + cg, 1), c=cg:
                  e.matmul(ps[3][0:1, G + c:G + c + 1], l, self.hfx("onesc", 0, 1),
                           start=True, stop=True),
                  inc="PE" if cg == G - 1 else None)
            pev = self.ms("pe")
            st_m = hfx("stats2", 0, 1, 0, 1)
            st_v = hfx("stats2", 1, 1, 0, 1)
            st_r = hfx("stats2", 2, 1, 0, 1)
            E("dv", lambda e: e.reduce_sum(out=st_m, in_=ps[3][0:1, 0:G], axis=AX),
              waits=[pev])
            self.drain("dv")
            E("dv", lambda e: e.reduce_sum(out=st_v, in_=ps[3][0:1, G:2 * G], axis=AX))
            self.drain("dv")
            E("dv", lambda e: e.tensor_scalar_mul(out=st_m, in0=st_m, scalar1=1.0 / D))
            self.drain("dv")
            E("dv", lambda e: e.tensor_mul(out=st_r, in0=st_m, in1=st_m))
            self.drain("dv")
            E("dv", lambda e: e.scalar_tensor_tensor(out=st_v, in0=st_v,
                                                     scalar=1.0 / D, in1=st_r,
                                                     op0=ALU.mult, op1=ALU.subtract),
              inc="DV")
            self.drain("dv")
            dvv = self.ms("dv")
            E("ac", lambda e: e.activation(out=st_v, in_=st_v, func=AF.Ln,
                                           bias=self.hfx("eps", 0, 1, 0, 1)),
              waits=[dvv])
            self.drain("ac")
            E("ac", lambda e: e.activation(out=st_v, in_=st_v, func=AF.Exp,
                                           scale=-0.5), inc="AC")
            self.drain("ac")
            acv = self.ms("ac")
            E("pe", lambda e: e.matmul(ps[3][:, 16:17],
                                       self.hfx("onesr", 0, 128, 0, 1), st_m,
                                       start=True, stop=True), waits=[acv])
            E("pe", lambda e: e.matmul(ps[3][:, 17:18],
                                       self.hfx("onesr", 0, 128, 0, 1), st_v,
                                       start=True, stop=True), inc="PE")
            pev = self.ms("pe")
            E("dv", lambda e: e.tensor_tensor(out=t2(8), in0=t2(8),
                                              in1=ps[3][:, 16:17].broadcast_to([128, G]),
                                              op=ALU.subtract), waits=[pev])
            self.drain("dv")
            E("dv", lambda e: e.tensor_tensor(out=t2(8), in0=t2(8),
                                              in1=ps[3][:, 17:18].broadcast_to([128, G]),
                                              op=ALU.mult))
            self.drain("dv")
            E("dv", lambda e: e.tensor_tensor(out=t2(8), in0=t2(8),
                                              in1=self.chps("g2"), op=ALU.mult))
            self.drain("dv")
            E("dv", lambda e: e.tensor_tensor(out=t2(8), in0=t2(8),
                                              in1=self.chps("b2"), op=ALU.add))
            self.drain("dv")
            E("dv", lambda e: e.tensor_scalar_mul(out=t2(8), in0=t2(8),
                                                  scalar1=self.hfx("hmask", 0, 1)))
            self.drain("dv")
            E("dv", lambda e: e.tensor_copy(out=hfx("small", 8, G), in_=t2(8)),
              inc="DV")
            self.drain("dv")
            self.war["ps2"] = self.ms("dv")
            self.war["ps3"] = self.ms("dv")

        bc = self.ln_stats(self.ar3("xmid"), self.hbx("uA", 0, 4096),
                           self.hbx("uA", 4096, 4096), self.rows_pb(), 2,
                           xmid_ms, sq_war=self.war.pop("uA", None))
        xa_war = self.war.pop("xa", None)
        E("dv", lambda e: e.tensor_copy(out=self.xa3(0, 1), in_=sm8(8)),
          waits=[xa_war] if xa_war else [])
        self.drain("dv")
        E("dv", lambda e: e.tensor_tensor(
            out=self.xa3(1, TT), in0=self.ar3("xmid"),
            in1=self.bc_row(ps[2][:, 0:TT]), op=ALU.subtract), waits=[bc])
        E("dv", lambda e: e.tensor_tensor(
            out=self.xa3(1, TT), in0=self.xa3(1, TT),
            in1=self.bc_row(ps[3][:, 0:TT]), op=ALU.mult))
        E("dv", lambda e: e.tensor_tensor(
            out=self.xa3(1, TT), in0=self.xa3(1, TT),
            in1=self.bc_grp(self.chps("g2")), op=ALU.mult))
        E("dv", lambda e: e.tensor_tensor(
            out=self.xa3(1, TT), in0=self.xa3(1, TT),
            in1=self.bc_grp(self.chps("b2")), op=ALU.add), inc="DV")
        self.war["ps2"] = self.ms("dv")
        self.war["ps3"] = self.ms("dv")
        E("dv", lambda e: e.tensor_copy(out=sm8(8), in_=self.xa3(TT, 1)))
        self.drain("dv")

        pj_war = self.war.pop("projbf", None)
        E("dv", lambda e: e.tensor_tensor(
            out=self.ar3("uA"), in0=self.xa3(1, TT), in1=self.xa3(0, TT),
            op=ALU.subtract), waits=[bc])
        self.drain("dv")
        for gi in range(G):
            w = [pj_war] if (gi == 0 and pj_war) else []
            E("dv", lambda e, d_=ar("uA", gi), x0=self.xav(gi, 0, TT),
              o=self.pbf(0, gi), s1=self.chpc("cmk", gi):
              e.scalar_tensor_tensor(out=o, in0=d_, scalar=s1, in1=x0,
                                     op0=ALU.mult, op1=ALU.add), waits=w)
        rw_war = self.war.pop("rwkvb", None)
        for gi in range(G):
            w = [rw_war] if (gi == 0 and rw_war) else []
            E("dv", lambda e, d_=ar("uA", gi), x0=self.xav(gi, 0, TT),
              o=self.rwb(gi), s1=self.chpc("cmr", gi):
              e.scalar_tensor_tensor(out=o, in0=d_, scalar=s1, in1=x0,
                                     op0=ALU.mult, op1=ALU.add),
              waits=w, inc="DV" if gi == G - 1 else None)
        mix2_ms = self.ms("dv")
        self.war["xa"] = mix2_ms

        q_war = self.war.pop("gbuf", None)
        for og in range(G):
            lw = self.ring_load(og, self.RG["fwr"] + og * G, G)
            bank = 6 + og % 2
            w0 = [lw, mix2_ms, self.war.pop(f"ps{bank}", (None, 0))]
            for kg in range(G):
                E("pe", lambda e, l=self.slot_blk(og, kg), r=self.rwb(kg),
                  b=bank, st=(kg == 0), sp=(kg == G - 1):
                  e.matmul(ps[b], l, r, start=st, stop=sp),
                  waits=w0 if kg == 0 else [],
                  inc="PE" if kg == G - 1 else None)
            self.ring_user[og] = self.ms("pe")
            w = [self.ms("pe")]
            if og == 0 and q_war:
                w.append(q_war)
            E("ac", lambda e, b=bank, o=self.gbuf(og):
              e.activation(out=o, in_=ps[b], func=AF.Sigmoid),
              waits=w, inc="AC")
            self.war[f"ps{bank}"] = self.ms("ac")
        gate_ms = self.ms("ac")
        self.war["rwkvb"] = self.ms("pe")
        return (mt, mix2_ms, gate_ms)

    def gen_pb_b(self, pend, stA):
        mt, xn_spill, rw_ms = pend
        _, mix2_ms, gate_ms = stA
        E, ar, hfx, ps = self.E, self.ar, self.hfx, self.psap

        for hc in range(HFG):
            si = hc % 8
            lw = self.ring_load(si, self.RG["fwk"] + hc * G, G,
                                extra=([self.p0_done("P0ffn")]
                                       if mt == 0 and hc == 0 else []))
            bank = 6 + hc % 2
            w0 = [lw, self.war.pop(f"ps{bank}", (None, 0))]
            if hc == 0:
                w0.append(mix2_ms)
            for kg in range(G):
                E("pe", lambda e, l=self.slot_blk(si, kg), r=self.pbf(0, kg),
                  b=bank, st=(kg == 0), sp=(kg == G - 1):
                  e.matmul(ps[b], l, r, start=st, stop=sp),
                  waits=w0 if kg == 0 else [],
                  inc="PE" if kg == G - 1 else None)
            self.ring_user[si] = self.ms("pe")
            sc = hc % 2
            w = [self.ms("pe"), self.war.pop(f"scr{sc}", (None, 0))]
            if hc == 16:
                w.append(self.war.pop("projbf1", (None, 0)))
            E("ac", lambda e, b=bank, o=hfx("scr", sc * TT, TT):
              e.activation(out=o, in_=ps[b], func=AF.Relu), waits=w, inc="AC")
            self.war[f"ps{bank}"] = self.ms("ac")
            E("pl", lambda e, i=hfx("scr", sc * TT, TT), o=self.kkb(hc):
              e.tensor_tensor(out=o, in0=i, in1=i, op=ALU.mult),
              waits=[self.ms("ac")], inc="PL")
            self.war[f"scr{sc}"] = self.ms("pl")
        kk_ms = self.ms("pl")
        self.war["projbf"] = self.ms("pe")

        for og in range(G):
            hs = 8 + og % 2
            lw0 = self.ring_load(hs, self.RG["fwv"] + og * HFG, 16)
            bank = 6 + og % 2
            w0 = [lw0, kk_ms, gate_ms, self.war.pop(f"ps{bank}", (None, 0))]
            for hk in range(HFG):
                if hk == 16:
                    lw1 = self.ring_load(hs, self.RG["fwv"] + og * HFG + 16, 16)
                    w1 = [lw1]
                else:
                    w1 = []
                E("pe", lambda e, l=self.slot_blk(hs, hk % 16), r=self.kkb(hk),
                  b=bank, st=(hk == 0), sp=(hk == HFG - 1):
                  e.matmul(ps[b], l, r, start=st, stop=sp),
                  waits=(w0 if hk == 0 else w1),
                  inc="PE" if hk in (15, HFG - 1) else None)
                if hk == 15:
                    self.ring_user[hs] = self.ms("pe")
            self.ring_user[hs] = self.ms("pe")
            pev = self.ms("pe")
            E("ac", lambda e, b=bank, o=hfx("scr2", 0, TT):
              e.copy(out=o, in_=ps[b]),
              waits=[pev, self.war.pop("scr2", (None, 0))], inc="AC")
            acv = self.ms("ac")
            self.war[f"ps{bank}"] = acv
            E("pl", lambda e, g_=self.gbuf(og), o=hfx("scr2", 0, TT):
              e.tensor_tensor(out=o, in0=o, in1=g_, op=ALU.mult),
              waits=([acv, gate_ms] if og == 0 else [acv]))
            self.drain("pl")
            E("pl", lambda e, o=ar("xmid", og), i=hfx("scr2", 0, TT):
              e.tensor_tensor(out=o, in0=o, in1=i, op=ALU.add), inc="PL")
            self.drain("pl")
            self.war["scr2"] = self.ms("pl")
        s_ms = self.ms("pl")
        out_ap = self.s_sc.rearrange("(g p) t -> p g t", p=128)[
            :, :, mt * TT:(mt + 1) * TT]
        E("sy", lambda e, o=out_ap,
          i=hfx("xmid", 0, G * TT).rearrange("p (g t) -> p g t", g=G):
          e.dma_start(out=o, in_=i), waits=[s_ms], inc="SS", inc_k=16)

        self.war["xmid"] = ("SS", self.cnt["SS"])
        self.war["gbuf"] = s_ms
        self.war["uA"] = self.ms("pe")
        self.war["projbf1"] = self.ms("pe")
        self.pump_p0(8)

    def accv(self, og, sub):
        reg = "xfm" if og < 4 else "dA"
        return self.hfx(reg, (og % 4) * 1024 + sub * TT, TT)

    def eob(self, sub, hc, n=TT, c0=0):
        reg = "xn" if sub == 0 else "uA"
        return self.hbx(reg, hc * TT + c0, n)

    def xnbv(self, gi, sub, n=TT, c0=0):
        return self.hbx("qA", gi * 1024 + sub * TT + c0, n)

    def gen_pc_st(self, u):
        E, ar, hfx, ps = self.E, self.ar, self.hfx, self.psap
        G_ = G
        onesr = self.hfx("onesr", 0, 128, 0, 1)
        lgsb = self.hfx("lgsb", 0, TT, 0, 8)

        top2_dv = []
        for sub in range(2):
            mt0 = u * 1024 + sub * TT
            w = [self.war.pop("xn_pc", (None, 0)),
                 ("SXN", self.cnt["SXN"])]
            in_ap = self.xn_sc.rearrange("(g p) t -> p g t", p=128)[
                :, :, mt0:mt0 + TT]
            E("sy", lambda e, i=in_ap,
              o=hfx("xn", 0, G_ * TT).rearrange("p (g t) -> p g t", g=G_):
              e.dma_start(out=o, in_=i), waits=w, inc="RXN", inc_k=16)
            rxn = ("RXN", self.cnt["RXN"])
            w0 = [rxn, self.war.pop("ps4", (None, 0))]
            if u == 0 and sub == 0:
                w0.append(("LC", self.LCv))
            for kg in range(G_):
                E("pe", lambda e, l=hfx("rwt", kg * 8, 8), r=ar("xn", kg),
                  st=(kg == 0), sp=(kg == G_ - 1):
                  e.matmul(ps[4][0:8, :], l, r, start=st, stop=sp),
                  waits=w0 if kg == 0 else [],
                  inc="PE" if kg == G_ - 1 else None)
            pev = self.ms("pe")
            E("dv", lambda e: e.tensor_scalar(
                out=lgsb, in0=ps[4][0:8, :], scalar1=self.hfx("ebias", 0, 1, 0, 8),
                scalar2=None, op0=ALU.add), waits=[pev], inc="DV")
            self.drain("dv")
            lg_ms = self.ms("dv")
            self.war["ps4"] = lg_ms
            for gi in range(G_):
                E("dv", lambda e, i=ar("xn", gi), o=self.xnbv(gi, sub):
                  e.tensor_copy(out=o, in_=i),
                  waits=[pev] if gi == 0 else [],
                  inc="DV" if gi == G_ - 1 else None)
            self.drain("dv")
            self.war["xn_pc"] = self.ms("dv")

            t2c = lambda c0, n: self.hfx("t2", c0, n)
            cwt_ps_w = [self.war.pop("ps5", (None, 0))]
            for c in range(4):
                E("pe", lambda e, i=lgsb[0:8, c * 128:(c + 1) * 128],
                  o=ps[5][:, c * 8:(c + 1) * 8]:
                  e.transpose(o, i, self.hfx("ident", 0, 8, 0, 8)),
                  waits=([lg_ms] + cwt_ps_w) if c == 0 else [],
                  inc="PE" if c == 3 else None)
            pev = self.ms("pe")
            for c in range(4):
                E("dv", lambda e, i=ps[5][:, c * 8:(c + 1) * 8],
                  o=self.hfx("lgT", c * 8, 8): e.tensor_copy(out=o, in_=i),
                  waits=[pev] if c == 0 else [],
                  inc="DV" if c == 3 else None)
            self.drain("dv")
            self.war["ps5"] = self.ms("dv")
            for c in range(4):
                lg = self.hfx("lgT", c * 8, 8)
                cw = self.hfx("cwt", c * 8, 8)
                m1 = t2c(32, 1); m2 = t2c(33, 1); r1 = t2c(34, 1); r2 = t2c(35, 1)
                eq1 = t2c(0, 8); eq2 = t2c(8, 8); l2 = t2c(16, 8)
                E("dv", lambda e, lg=lg, m1=m1: e.reduce_max(out=m1, in_=lg, axis=AX))
                self.drain("dv")
                E("dv", lambda e, lg=lg, m1=m1, eq1=eq1: e.tensor_scalar(
                    out=eq1, in0=lg, scalar1=m1, scalar2=None, op0=ALU.is_equal))
                self.drain("dv")
                E("dv", lambda e, lg=lg, eq1=eq1, l2=l2: e.scalar_tensor_tensor(
                    out=l2, in0=eq1, scalar=-1e9, in1=lg, op0=ALU.mult, op1=ALU.add))
                self.drain("dv")
                E("dv", lambda e, l2=l2, m2=m2: e.reduce_max(out=m2, in_=l2, axis=AX))
                self.drain("dv")
                E("dv", lambda e, l2=l2, m2=m2, eq2=eq2: e.tensor_scalar(
                    out=eq2, in0=l2, scalar1=m2, scalar2=None, op0=ALU.is_equal))
                self.drain("dv")
                E("dv", lambda e, m1=m1, m2=m2, d_=t2c(36, 1): e.tensor_tensor(
                    out=d_, in0=m1, in1=m2, op=ALU.subtract), inc="DV")
                self.drain("dv")
                E("ac", lambda e, d_=t2c(36, 1), r1=r1: e.activation(
                    out=r1, in_=d_, func=AF.Tanh, scale=0.5),
                  waits=[self.ms("dv")], inc="AC")
                self.drain("ac")
                E("dv", lambda e, r1=r1: e.tensor_scalar(
                    out=r1, in0=r1, scalar1=0.5, scalar2=0.5,
                    op0=ALU.mult, op1=ALU.add), waits=[self.ms("ac")])
                self.drain("dv")
                E("dv", lambda e, r1=r1, r2=r2: e.tensor_scalar(
                    out=r2, in0=r1, scalar1=-1.0, scalar2=1.0,
                    op0=ALU.mult, op1=ALU.add))
                self.drain("dv")
                E("dv", lambda e, eq1=eq1, r1=r1, cw=cw: e.tensor_scalar(
                    out=cw, in0=eq1, scalar1=r1, scalar2=None, op0=ALU.mult))
                self.drain("dv")
                E("dv", lambda e, eq2=eq2, r2=r2, l2=l2: e.tensor_scalar(
                    out=l2, in0=eq2, scalar1=r2, scalar2=None, op0=ALU.mult))
                self.drain("dv")
                E("dv", lambda e, cw=cw, l2=l2: e.tensor_tensor(
                    out=cw, in0=cw, in1=l2, op=ALU.add),
                  inc="DV" if c == 3 else None)
                self.drain("dv")
            cw_ms = self.ms("dv")
            w0 = [cw_ms, self.war.pop("ps5", (None, 0))]
            for c in range(4):
                E("pe", lambda e, i=self.hfx("cwt", c * 8, 8),
                  o=ps[5][0:8, c * 128:(c + 1) * 128]:
                  e.transpose(o, i, self.hfx("ident", 0, 128)),
                  waits=w0 if c == 0 else [], inc="PE" if c == 3 else None)
            pev = self.ms("pe")
            cwT = self.hfx("cwrow", 0, TT, 0, 8)
            E("dv", lambda e, o=cwT: e.tensor_copy(out=o, in_=ps[5][0:8, :]),
              waits=[pev], inc="DV")
            self.drain("dv")
            self.war["ps5"] = self.ms("dv")
            E("sy", lambda e, o=self.cw_sc[:, sub * TT:(sub + 1) * TT], i=cwT:
              e.dma_start(out=o, in_=i), waits=[self.ms("dv")],
              inc="SCW", inc_k=16)
            top2_dv.append(("SCW", self.cnt["SCW"]))

        if PC_PARTS < 2:
            return
        for ex in range(PC_NEX):
            shared = ex == NEX - 1
            if not shared:
                for sub in range(2):
                    rsem = f"RCW{sub}"
                    w = [top2_dv[sub], self.war.pop("cwrow", (None, 0))]
                    E("sy", lambda e, i=self.cw_sc[ex:ex + 1, sub * TT:(sub + 1) * TT],
                      o=self.hfx("cwrow", TT, TT, 0, 1):
                      e.dma_start(out=o, in_=i), waits=w, inc=rsem, inc_k=16)
                    rv = (rsem, self.cnt[rsem])
                    w0 = [rv, self.war.pop("ps4", (None, 0))]
                    E("pe", lambda e, r=self.hfx("cwrow", TT, TT, 0, 1):
                      e.matmul(ps[4], onesr, r, start=True, stop=True),
                      waits=w0, inc="PE")
                    pev = self.ms("pe")
                    self.war["cwrow"] = pev
                    E("dv", lambda e, o=self.hfx("cwb", sub * TT, TT):
                      e.tensor_copy(out=o, in_=ps[4]),
                      waits=[pev, self.war.pop(f"cwb{sub}", (None, 0)),
                             self.war.pop("gbuf", (None, 0))], inc="DV")
                    self.drain("dv")
                    self.war["ps4"] = self.ms("dv")
            cwb_ms = self.ms("dv")

            for hc in range(HMG):
                s1 = self.ring_load(hc % 4, self.RG["moe1"] + (ex * HMG + hc) * G_,
                                    G_, extra=([self.p0_done("P0moe")]
                                               if u == 0 and ex == 0 and hc < 4
                                               else []))
                s3 = self.ring_load(4 + hc % 4, self.RG["moe3"] + (ex * HMG + hc) * G_,
                                    G_)
                for sub in range(2):
                    bank_g, bank_u = sub, 2 + sub
                    w0 = [s1, self.war.pop(f"ps{bank_g}", (None, 0))]
                    if hc == 0 and sub == 0:
                        w0.append(cwb_ms)
                        w0.append(self.war.pop("eo", (None, 0)))
                    for kg in range(G_):
                        E("pe", lambda e, l=self.slot_blk(hc % 4, kg),
                          r=self.xnbv(kg, sub), b=bank_g,
                          st=(kg == 0), sp=(kg == G_ - 1):
                          e.matmul(ps[b], l, r, start=st, stop=sp),
                          waits=w0 if kg == 0 else [],
                          inc="PE" if kg == G_ - 1 else None)
                    w0 = [s3, self.war.pop(f"ps{bank_u}", (None, 0))]
                    for kg in range(G_):
                        E("pe", lambda e, l=self.slot_blk(4 + hc % 4, kg),
                          r=self.xnbv(kg, sub), b=bank_u,
                          st=(kg == 0), sp=(kg == G_ - 1):
                          e.matmul(ps[b], l, r, start=st, stop=sp),
                          waits=w0 if kg == 0 else [],
                          inc="PE" if kg == G_ - 1 else None)
                    pev = self.ms("pe")
                    if sub == 1:
                        self.ring_user[hc % 4] = pev
                        self.ring_user[4 + hc % 4] = pev
                    sc = self.hfx("scr", sub * TT, TT)
                    E("ac", lambda e, b=bank_g, o=sc:
                      e.activation(out=o, in_=ps[b], func=AF.Silu),
                      waits=[pev, self.war.pop(f"scr{sub}", (None, 0))], inc="AC")
                    acv = self.ms("ac")
                    self.war[f"ps{bank_g}"] = acv
                    if shared:
                        E("dv", lambda e, b=bank_u, i=sc, o=self.eob(sub, hc):
                          e.tensor_tensor(out=o, in0=i, in1=ps[b], op=ALU.mult),
                          waits=[acv], inc="DV")
                        self.drain("dv")
                    else:
                        sc2 = self.hfx("scr2", 0, TT)
                        E("dv", lambda e, b=bank_u, i=sc, o=sc2:
                          e.tensor_tensor(out=o, in0=i, in1=ps[b], op=ALU.mult),
                          waits=[acv], inc="DV")
                        self.drain("dv")
                        E("dv", lambda e, i=sc2, cwb=self.hfx("cwb", sub * TT, TT),
                          o=self.eob(sub, hc):
                          e.tensor_tensor(out=o, in0=i, in1=cwb, op=ALU.mult),
                          inc="DV")
                        self.drain("dv")
                    dvv = self.ms("dv")
                    self.war[f"ps{bank_u}"] = dvv
                    self.war[f"scr{sub}"] = dvv
            eo_ms = self.ms("dv")
            self.war[f"cwb0"] = eo_ms
            self.war[f"cwb1"] = eo_ms

            for og in range(G_):
                s2s = self.ring_load(8 + og % 2, self.RG["moe2"] + (ex * G_ + og) * HMG,
                                     HMG)
                for sub in range(2):
                    bank = 4 + 2 * (og % 2) + sub
                    w0 = [s2s, eo_ms, self.war.pop(f"ps{bank}", (None, 0))]
                    for hk in range(HMG):
                        E("pe", lambda e, l=self.slot_blk(8 + og % 2, hk),
                          r=self.eob(sub, hk), b=bank,
                          st=(hk == 0), sp=(hk == HMG - 1):
                          e.matmul(ps[b], l, r, start=st, stop=sp),
                          waits=w0 if hk == 0 else [],
                          inc="PE" if hk == HMG - 1 else None)
                    pev = self.ms("pe")
                    if sub == 1:
                        self.ring_user[8 + og % 2] = pev
                    if ex == 0:
                        E("dv", lambda e, b=bank, o=self.accv(og, sub):
                          e.tensor_copy(out=o, in_=ps[b]),
                          waits=[pev] + ([self.war.pop("acc", (None, 0))]
                                         if og == 0 and sub == 0 else []),
                          inc="DV")
                    else:
                        E("dv", lambda e, b=bank, o=self.accv(og, sub):
                          e.tensor_tensor(out=o, in0=o, in1=ps[b], op=ALU.add),
                          waits=[pev], inc="DV")
                    self.drain("dv")
                    self.war[f"ps{bank}"] = self.ms("dv")
            self.war["eo"] = self.ms("pe")
        acc_ms = self.ms("dv")
        if PC_NEX < NEX:
            return

        if PC_PARTS < 3:
            return
        for sub in range(2):
            mt0 = u * 1024 + sub * TT
            w = [("SS", self.cnt["SS"]), self.war.pop("s_pc", (None, 0))]
            in_ap = self.s_sc.rearrange("(g p) t -> p g t", p=128)[
                :, :, mt0:mt0 + TT]
            E("sy", lambda e, i=in_ap,
              o=hfx("projbf", 0, G_ * TT).rearrange("p (g t) -> p g t", g=G_):
              e.dma_start(out=o, in_=i), waits=w, inc="RSS", inc_k=16)
            rss = ("RSS", self.cnt["RSS"])
            for og in range(G_):
                sc = self.hfx("scr", (og % 2) * TT, TT)
                E("ac", lambda e, i=self.accv(og, sub), o=sc:
                  e.activation(out=o, in_=i, func=AF.Silu),
                  waits=[acc_ms, self.war.pop(f"scr{og % 2}", (None, 0))]
                  + ([self.war.pop("xa_pc", (None, 0))] if og == 0 else []),
                  inc="AC")
                acv = self.ms("ac")
                E("dv", lambda e, i=sc, s_=hfx("projbf", og * TT, TT),
                  o=self.xap(og): e.tensor_tensor(out=o, in0=i, in1=s_,
                                                  op=ALU.mult),
                  waits=[acv, rss] if og == 0 else [acv], inc="DV")
                self.drain("dv")
                self.war[f"scr{og % 2}"] = self.ms("dv")
            t_ms = self.ms("dv")

            for a in range(4):
                half = a % 2
                w = [self.war.pop(f"xtm_pc{half}", (None, 0)),
                     self.war.pop(f"xtm{half}", (None, 0)),
                     self.war.pop("gbuf", (None, 0))]
                E("sy", lambda e, r0=TOK_W + mt0 + a * 128,
                  d=hfx("xtm", half * 1024, 1024):
                  e.dma_start(out=d, in_=self.xe[r0:r0 + 128, :]),
                  waits=w, inc="RX2", inc_k=16)
                rx2 = ("RX2", self.cnt["RX2"])
                for og in range(G_):
                    bank = 2 * half + og // 4
                    w0 = []
                    if og == 0:
                        w0 = [t_ms, self.war.pop(f"ps{2 * half}", (None, 0)),
                              self.war.pop(f"ps{2 * half + 1}", (None, 0))]
                    E("pe", lambda e, i=self.xap(og, 128, a * 128),
                      o=ps[bank][:, (og % 4) * 128:(og % 4) * 128 + 128]:
                      e.transpose(o, i, self.hfx("ident", 0, 128)),
                      waits=w0, inc="PE" if og == G_ - 1 else None)
                pev = self.ms("pe")
                for og in range(G_):
                    bank = 2 * half + og // 4
                    w0 = [pev, rx2] if og == 0 else []
                    if og == 0:
                        w0.append(self.war.pop(f"outtm{half}", (None, 0)))
                    E("dv", lambda e, p_=ps[bank][:, (og % 4) * 128:(og % 4) * 128 + 128],
                      x_=hfx("xtm", half * 1024 + og * 128, 128),
                      o=hfx("p0in", half * 1024 + og * 128, 128):
                      e.tensor_tensor(out=o, in0=x_, in1=p_, op=ALU.add),
                      waits=w0, inc="DV" if og == G_ - 1 else None)
                dvv = self.ms("dv")
                self.war[f"ps{2 * half}"] = dvv
                self.war[f"ps{2 * half + 1}"] = dvv
                self.war[f"xtm_pc{half}"] = dvv
                E("sy", lambda e, o=self.y[mt0 + a * 128: mt0 + a * 128 + 128, :],
                  i=hfx("p0in", half * 1024, 1024):
                  e.dma_start(out=o, in_=i), waits=[dvv], inc="SY", inc_k=16)
                self.war[f"outtm{half}"] = ("SY", self.cnt["SY"])
            self.war["xa_pc"] = self.ms("dv")
            self.war["s_pc"] = self.ms("dv")
        self.war["acc"] = self.ms("ac")
        self.war["xn_pc"] = self.ms("pe")

    def finalize_streams(self, extra_final_waits=()):
        for s, v in extra_final_waits:
            self.stream["sy"].append(lambda e, s=s, v=v: e.wait_ge(self.S[s], v))

        blk = ExitStack()
        block = blk.enter_context(self.nc.Block())
        for eng_name, attr in [("sy", "sync"), ("pe", "tensor"), ("dv", "vector"),
                               ("ac", "scalar"), ("pl", "gpsimd")]:
            items = self.stream[eng_name]

            def make(items):
                def body(e):
                    for f in items:
                        f(e)
                return body
            getattr(block, attr)(make(items))
        blk.close()


def build_full(pc=True):
    mk = MK()
    ctx = ExitStack()
    mk.enter(ctx)
    mk.gen_p0()
    mk.gen_startup()
    mk.emit_p0()
    pend = None
    for t in range(NT):
        if pend is not None:
            stA = mk.gen_pb_a(pend)
        bc = mk.gen_pa_head(t)
        if pend is not None:
            mk.gen_pb_b(pend, stA)
        pend = mk.gen_pa_tail(t, bc)
    if pend is not None:
        stA = mk.gen_pb_a(pend)
        mk.gen_pb_b(pend, stA)
    finals = [("SXN", mk.cnt["SXN"]), ("SS", mk.cnt["SS"])]
    if pc:
        for u in range(PC_STS):
            mk.gen_pc_st(u)
        if PC_STS < 4 or PC_PARTS < 3:
            mk.E("dv", lambda e: e.memset(mk.hfx("xtm", 0, 1024), 0.0), inc="DV")
            mk.drain("dv")
            dv = mk.ms("dv")
            for r0 in range(0, TOK_M, 128):
                mk.E("sy", lambda e, r=r0: e.dma_start(out=mk.y[r:r + 128, :],
                                                       in_=mk.hfx("xtm", 0, 1024)),
                     waits=[dv] if r0 == 0 else [], inc="SY", inc_k=16)
        finals.append(("SY", mk.cnt["SY"]))
    else:
        mk.E("dv", lambda e: e.memset(mk.hfx("xtm", 0, 1024), 0.0), inc="DV")
        mk.drain("dv")
        dv = mk.ms("dv")
        for r0 in range(0, TOK_M, 128):
            mk.E("sy", lambda e, r=r0: e.dma_start(out=mk.y[r:r + 128, :],
                                                   in_=mk.hfx("xtm", 0, 1024)),
                 waits=[dv] if r0 == 0 else [], inc="SY", inc_k=16)
        finals.append(("SY", mk.cnt["SY"]))
    mk.finalize_streams(finals)
    return mk, ctx


def kernel(**inputs):
    in_maps = host_prep(inputs)
    mk, ctx = build_full(pc=True)
    try:
        res = run_bass_kernel_spmd(mk.nc, in_maps, core_ids=list(range(NCORES)))
    finally:
        ctx.close()
    out = np.empty((2, 16384, D), np.float32)
    for c in range(NCORES):
        b, s = c // 4, c % 4
        out[b, s * TOK_M:(s + 1) * TOK_M] = res.results[c]["y"]
    return out


# version 07 (final)
# speedup vs baseline: 1.0759x; 1.0759x over previous
import numpy as np
from contextlib import ExitStack

import concourse.bass as bass
import concourse.mybir as mybir
from concourse.bass_utils import run_bass_kernel_spmd

F32 = mybir.dt.float32
BF16 = mybir.dt.bfloat16
AF = mybir.ActivationFunctionType
ALU = mybir.AluOpType
AX = mybir.AxisListType.X

G = 8
TT = 512
WM_TILES, MAIN_TILES = 4, 8
NT = WM_TILES + MAIN_TILES
TOK_W, TOK_M = WM_TILES * TT, MAIN_TILES * TT
TOK_E = TOK_W + TOK_M
D = 1024
HFG = 32
HMG = 16
NEX = 9
NCORES = 8
LNEPS = 1e-5
NBLK = 5 * G * G + HFG * G + G * HFG + NEX * HMG * G * 2 + NEX * G * HMG

DBG = []
PC_STS = 4
PC_PARTS = 3
PC_NEX = NEX


def _blk(a, KG, MG):
    K, M = a.shape
    assert K == KG * 128 and M == MG * 128
    return np.ascontiguousarray(a.reshape(KG, 128, MG, 128).transpose(2, 0, 1, 3))


def host_prep(inputs):
    f = np.float32
    x = np.asarray(inputs["L0_attn_x"], f)
    blocks = []
    add = lambda arr: blocks.append(arr.reshape(-1, 128, 128))
    add(_blk(np.ascontiguousarray(np.asarray(inputs["L0_attn_att_Wk"], f).T), G, G))
    add(_blk(np.ascontiguousarray(np.asarray(inputs["L0_attn_att_Wv"], f).T), G, G))
    add(_blk(np.ascontiguousarray(np.asarray(inputs["L0_attn_att_Wr"], f).T), G, G))
    add(_blk(np.ascontiguousarray(np.asarray(inputs["L0_attn_att_Wo"], f).T), G, G))
    add(_blk(np.ascontiguousarray(np.asarray(inputs["L0_attn_ffn_Wr"], f).T), G, G))
    add(_blk(np.ascontiguousarray(np.asarray(inputs["L0_attn_ffn_Wk"], f).T), G, HFG))
    add(_blk(np.ascontiguousarray(np.asarray(inputs["L0_attn_ffn_Wv"], f).T), HFG, G))
    w1 = np.concatenate([np.asarray(inputs["L0_ffn_w1"], f),
                         np.asarray(inputs["L0_ffn_sw1"], f)], 0)
    w3 = np.concatenate([np.asarray(inputs["L0_ffn_w3"], f),
                         np.asarray(inputs["L0_ffn_sw3"], f)], 0)
    w2 = np.concatenate([np.asarray(inputs["L0_ffn_w2"], f),
                         np.asarray(inputs["L0_ffn_sw2"], f)], 0)
    add(np.stack([_blk(w1[e], G, HMG) for e in range(NEX)]))
    add(np.stack([_blk(w3[e], G, HMG) for e in range(NEX)]))
    add(np.stack([_blk(w2[e], HMG, G) for e in range(NEX)]))
    wall = np.ascontiguousarray(np.concatenate(blocks, 0)).reshape(NBLK, 128 * 128)

    names = ["time_decay", "time_first", "time_mix_k", "time_mix_v",
             "time_mix_r", "cm_k", "cm_r", "ln1_g", "ln1_b", "ln2_g", "ln2_b"]
    chp = np.ascontiguousarray(np.concatenate(
        [np.asarray(inputs["L0_attn_" + n], f).reshape(G, 128).T for n in names], 1))
    rwtb = np.ascontiguousarray(
        np.asarray(inputs["L0_ffn_router_w"], f).T.reshape(G, 128, 8))
    ebias = np.asarray(inputs["L0_ffn_experts_bias"], f).reshape(8, 1)
    ident = np.eye(128, dtype=f)
    onesc = np.ones((128, 1), f)
    onesr = np.ones((1, 128), f)
    in_maps = []
    for c in range(NCORES):
        b, s = c // 4, c % 4
        t0 = s * TOK_M
        if s == 0:
            xcore = np.concatenate([np.zeros((TOK_W, D), f), x[b, 0:TOK_M]], 0)
            hm = np.zeros((128, 1), f)
        else:
            xcore = np.ascontiguousarray(x[b, t0 - TOK_W: t0 + TOK_M])
            hm = np.ones((128, 1), f)
        in_maps.append(dict(xe=xcore, wall=wall, chp=chp, rwtb=rwtb, ebias=ebias,
                            ident=ident, onesc=onesc, onesr=onesr, hmask=hm))
    return in_maps


class MK:
    def __init__(self):
        nc = bass.Bass("TRN2", target_bir_lowering=False)
        self.nc = nc
        self.stream = {e: [] for e in ("pe", "dv", "ac", "pl", "sy")}
        self.cnt = {}
        self.S = {}
        self.semnames = []
        self.war = {}
        self.ring_user = {}
        self.declare_dram()
        self.layout_heap()

    def sem(self, n):
        if n not in self.cnt:
            self.cnt[n] = 0
            self.semnames.append(n)
        return n

    def bump(self, n, k=1):
        self.cnt[n] = self.cnt.get(n, 0) + k
        return self.cnt[n]

    def E(self, eng, fn, waits=(), inc=None, inc_k=1):
        if inc:
            self.bump(inc, inc_k)
        S = self.S
        ws = tuple((s, v) for (s, v) in waits if v and v > 0)

        def closure(e, fn=fn, ws=ws, inc=inc, inc_k=inc_k):
            for s, v in ws:
                e.wait_ge(S[s], v)
            ins = fn(e)
            if inc is not None and ins is not None:
                ins.then_inc(S[inc], inc_k)
        self.stream[eng].append(closure)
        return (inc, self.cnt[inc]) if inc else None

    def drain(self, eng):
        self.stream[eng].append(lambda e: e.drain())

    def ms(self, eng):
        n = {"pe": "PE", "dv": "DV", "ac": "AC", "pl": "PL"}[eng]
        return (n, self.cnt.get(n, 0))

    def declare_dram(self):
        nc = self.nc
        self.xe = nc.dram_tensor("xe", [TOK_E, D], F32, kind="ExternalInput").ap()
        self.wall = nc.dram_tensor("wall", [NBLK, 128 * 128], F32,
                                   kind="ExternalInput").ap()
        self.chp_d = nc.dram_tensor("chp", [128, 11 * G], F32,
                                    kind="ExternalInput").ap()
        self.rwtb_d = nc.dram_tensor("rwtb", [G, 128, 8], F32,
                                     kind="ExternalInput").ap()
        self.ebias_d = nc.dram_tensor("ebias", [8, 1], F32,
                                      kind="ExternalInput").ap()
        self.ident_d = nc.dram_tensor("ident", [128, 128], F32,
                                      kind="ExternalInput").ap()
        self.onesc_d = nc.dram_tensor("onesc", [128, 1], F32,
                                      kind="ExternalInput").ap()
        self.onesr_d = nc.dram_tensor("onesr", [1, 128], F32,
                                      kind="ExternalInput").ap()
        self.hmask_d = nc.dram_tensor("hmask", [128, 1], F32,
                                      kind="ExternalInput").ap()
        self.y = nc.dram_tensor("y", [TOK_M, D], F32, kind="ExternalOutput").ap()
        ik = lambda nm: "ExternalOutput" if nm in DBG else "Internal"
        self.wbf = nc.dram_tensor("wbf", [NBLK, 128 * 128], BF16,
                                  kind="Internal").ap()
        self.xn_sc = nc.dram_tensor("xn_sc", [D, TOK_M], F32, kind=ik("xn_sc")).ap()
        self.s_sc = nc.dram_tensor("s_sc", [D, TOK_M], F32, kind=ik("s_sc")).ap()
        self.cw_sc = nc.dram_tensor("cw_sc", [NEX - 1, 2 * TT], F32,
                                    kind=ik("cw_sc")).ap()
        self.n0_dbg = (nc.dram_tensor("n0_dbg", [D, TOK_M], F32,
                                      kind="ExternalOutput").ap()
                       if "n0_dbg" in DBG else None)
        self.d0_dbg = (nc.dram_tensor("d0_dbg", [D, TOK_M], F32,
                                      kind="ExternalOutput").ap()
                       if "d0_dbg" in DBG else None)
        self.rw_dbg = (nc.dram_tensor("rw_dbg", [D, TOK_M], F32,
                                      kind="ExternalOutput").ap()
                       if "rw_dbg" in DBG else None)
        self.gk_dbg = (nc.dram_tensor("gk_dbg", [D, TOK_M], F32,
                                      kind="ExternalOutput").ap()
                       if "gk_dbg" in DBG else None)
        self.q_dbg = (nc.dram_tensor("q_dbg", [D, TOK_M], F32,
                                     kind="ExternalOutput").ap()
                      if "q_dbg" in DBG else None)
        self.u_dbg = (nc.dram_tensor("u_dbg", [D, TOK_M], F32,
                                     kind="ExternalOutput").ap()
                      if "u_dbg" in DBG else None)
        self.RG = {}
        rb = 0
        for nm, n in [("att_k", G * G), ("att_v", G * G), ("att_r", G * G),
                      ("wo", G * G), ("fwr", G * G), ("fwk", HFG * G),
                      ("fwv", G * HFG), ("moe1", NEX * HMG * G),
                      ("moe3", NEX * HMG * G), ("moe2", NEX * G * HMG)]:
            self.RG[nm] = rb
            rb += n
        assert rb == NBLK

    def layout_heap(self):
        self.off = {}
        top = [0]

        def al(name, cols):
            self.off[name] = top[0]
            top[0] += cols
        al("chp", 11 * G); al("der", 3 * G); al("ident", 128); al("onesc", 1)
        al("onesr", 128); al("ebias", 1); al("hmask", 1); al("rwt", G * 8)
        al("stats", 2 * TT); al("stats2", 2 * TT); al("small", 64); al("rwhb", 8); al("eps", 1); al("onesb", 1)
        al("lgsb", TT); al("lgT", 64); al("cwt", 64); al("t2", 64)
        al("cwrow", 2 * TT); al("cwb", 2 * TT)
        al("scr", 2 * TT); al("scr2", TT)
        al("p0in", 2 * 1024); al("p0out", 1024)
        al("xtm", 2 * 1024)
        al("ring", 6144)
        al("projbf", 2 * 2048)
        al("rwkvb", 2048)
        al("xmid", G * TT); al("xfm", G * TT); al("xn", G * TT); al("xa", G * (TT + 1))
        al("dA", G * TT); al("qA", G * TT); al("uA", G * TT)
        self.HEAPC = top[0]
        assert self.HEAPC <= 53100, self.HEAPC

    def enter(self, ctx):
        nc = self.nc
        self.heap = ctx.enter_context(
            nc.sbuf_tensor("heap", [128, self.HEAPC], F32)).ap()
        self.heapb = self.heap.bitcast(BF16)
        self.ps = [ctx.enter_context(nc.psum_tensor(f"ps{i}", [128, TT], F32))
                   for i in range(8)]
        self.psap = [p.ap() for p in self.ps]
        for nm in ["PE", "DV", "AC", "PL", "LC", "LX0", "LX1",
                   "P0I0", "P0I1", "P0att", "P0wo", "P0ffn", "P0moe",
                   "SXN", "SRW", "SS", "SCW", "SY",
                   "RXN", "RSS", "RCW0", "RCW1", "RX2"]:
            self.sem(nm)
        for i in range(10):
            self.sem(f"RS{i}")
        for n in self.semnames:
            self.S[n] = ctx.enter_context(nc.semaphore(n))

    def hfx(self, name, c0, n, p0=0, np_=128):
        o = self.off[name] + c0
        return self.heap[p0:p0 + np_, o:o + n]

    def hbx(self, name, c0, n, p0=0, np_=128):
        o = 2 * self.off[name] + c0
        return self.heapb[p0:p0 + np_, o:o + n]

    def ar(self, name, gi, n=TT, c0=0):
        return self.hfx(name, gi * TT + c0, n)

    def xav(self, gi, c0, n):
        return self.hfx("xa", gi * (TT + 1) + c0, n)

    def xap(self, gi, n=TT, c0=0):
        return self.hfx("xa", gi * TT + c0, n)

    def pbf(self, buf, gi, n=TT, c0=0):
        return self.hbx("projbf", buf * 4096 + gi * TT + c0, n)

    def rwb(self, gi, n=TT, c0=0):
        return self.hbx("rwkvb", gi * TT + c0, n)

    def gbuf(self, og, n=TT, c0=0):
        if og < 4:
            return self.hfx("p0in", og * TT + c0, n)
        if og < 6:
            return self.hfx("p0out", (og - 4) * TT + c0, n)
        return self.hfx("cwb", (og - 6) * TT + c0, n)

    def kkb(self, hc, n=TT, c0=0):
        reg = "xfm" if hc < 16 else "dA"
        return self.hbx(reg, (hc % 16) * TT + c0, n)

    def chpc(self, j, gi):
        CHI = dict(decay=0, tf=1, tmk=2, tmv=3, tmr=4, cmk=5, cmr=6,
                   g1=7, b1=8, g2=9, b2=10)
        return self.hfx("chp", CHI[j] * G + gi, 1)

    def chps(self, j):
        CHI = dict(decay=0, tf=1, tmk=2, tmv=3, tmr=4, cmk=5, cmr=6,
                   g1=7, b1=8, g2=9, b2=10)
        return self.hfx("chp", CHI[j] * G, G)

    def derc(self, j, gi):
        return self.hfx("der", dict(w=0, alpha=1, etf=2)[j] * G + gi, 1)

    def derg(self, j):
        return self.hfx("der", dict(w=0, alpha=1, etf=2)[j] * G, G)

    def small(self, c0, n=1):
        return self.hfx("small", c0, n)


    def ar3(self, name, c0=0, n=TT):
        a = self.hfx(name, 0, G * TT)
        return a.rearrange("p (g t) -> p g t", g=G)[:, :, c0:c0 + n]

    def xa3(self, c0, n):
        a = self.hfx("xa", 0, G * (TT + 1))
        return a.rearrange("p (g t) -> p g t", g=G, t=TT + 1)[:, :, c0:c0 + n]

    def bc_row(self, ap2, n=TT):
        return ap2.unsqueeze(1).broadcast_to([128, G, n])

    def bc_grp(self, ap2, n=TT):
        return ap2.unsqueeze(2).broadcast_to([128, G, n])

    def pb3(self, buf, c0=0, n=TT):
        b0 = 2 * self.off["projbf"] + buf * 4096
        a = self.heapb[:, b0:b0 + 4096]
        return a.rearrange("p (g t) -> p g t", g=G)[:, :, c0:c0 + n]

    def rw3(self, c0=0, n=TT):
        b0 = 2 * self.off["rwkvb"]
        a = self.heapb[:, b0:b0 + 4096]
        return a.rearrange("p (g t) -> p g t", g=G)[:, :, c0:c0 + n]

    def slot_blk(self, si, bi):
        if si < 8:
            c0 = 2 * (self.off["ring"] + si * 512) + bi * 128
        else:
            c0 = 2 * (self.off["ring"] + 4096 + (si - 8) * 1024) + bi * 128
        return self.heapb[:, c0:c0 + 128]

    def slot_ap(self, si, nblk):
        if si < 8:
            c0 = 2 * (self.off["ring"] + si * 512)
        else:
            c0 = 2 * (self.off["ring"] + 4096 + (si - 8) * 1024)
        return self.heapb[:, c0:c0 + nblk * 128]

    def ring_load(self, si, row0, nblk, extra=()):
        sname = self.sem(f"RS{si}")
        waits = list(extra)
        if si in self.ring_user:
            waits.append(self.ring_user.pop(si))
        self.E("sy", lambda e, a=self.wbf[row0:row0 + nblk, :]
               .rearrange("b (p q) -> p b q", p=128),
               s=self.slot_ap(si, nblk).rearrange("p (b q) -> p b q", q=128):
               e.dma_start(out=s, in_=a),
               waits=waits, inc=sname, inc_k=16)
        return (sname, self.cnt[sname])

    def gen_p0(self):
        self.P0_TOT = {}
        self.p0_jobs = []
        sizes = {"att_k": G * G, "att_v": G * G, "att_r": G * G,
                 "wo": G * G, "fwr": G * G, "fwk": HFG * G, "fwv": G * HFG,
                 "moe1": NEX * HMG * G, "moe3": NEX * HMG * G,
                 "moe2": NEX * G * HMG}
        for nm, rsem, splits in [("att_k", "P0att", 1), ("att_v", "P0att", 1),
                                 ("att_r", "P0att", 1), ("wo", "P0wo", 1),
                                 ("fwr", "P0wo", 1), ("fwk", "P0ffn", 1),
                                 ("fwv", "P0ffn", 1), ("moe1", "P0moe", 4),
                                 ("moe3", "P0moe", 4), ("moe2", "P0moe", 4)]:
            r0 = self.RG[nm]
            per = sizes[nm] // splits
            for sidx in range(splits):
                self.p0_jobs.append((r0 + sidx * per, per, rsem))
                self.P0_TOT[rsem] = self.P0_TOT.get(rsem, 0) + 1

    def emit_p0(self):
        for row0, nrows, rsem in self.p0_jobs:
            self.E("pl", lambda e, o=self.wbf[row0:row0 + nrows, :],
                   i=self.wall[row0:row0 + nrows, :]:
                   e.dma_start(out=o, in_=i), inc=rsem, inc_k=16)

    def pump_p0(self, n):
        pass

    def p0_done(self, rsem):
        return (rsem, 16 * self.P0_TOT[rsem])

    def gen_startup(self):
        for src, dst in [
                (self.chp_d, self.hfx("chp", 0, 11 * G)),
                (self.ident_d, self.hfx("ident", 0, 128)),
                (self.onesc_d, self.hfx("onesc", 0, 1)),
                (self.onesr_d, self.hfx("onesr", 0, 128, 0, 1)),
                (self.ebias_d, self.hfx("ebias", 0, 1, 0, 8)),
                (self.hmask_d, self.hfx("hmask", 0, 1)),
                (self.rwtb_d.rearrange("k p c -> p k c"),
                 self.hfx("rwt", 0, G * 8).rearrange("p (k c) -> p k c", k=G))]:
            self.E("sy", lambda e, s=src, d=dst: e.dma_start(out=d, in_=s),
                   inc="LC", inc_k=16)
        self.LCv = self.cnt["LC"]
        LC = ("LC", self.LCv)
        self.E("ac", lambda e: e.activation(out=self.derg("w"),
                                            in_=self.hfx("chp", 0, G),
                                            func=AF.Exp), waits=[LC], inc="AC")
        self.drain("ac")
        self.E("dv", lambda e: e.tensor_scalar_mul(out=self.derg("w"),
                                                   in0=self.derg("w"),
                                                   scalar1=-1.0),
               waits=[self.ms("ac")], inc="DV")
        self.drain("dv")
        self.E("ac", lambda e: e.activation(out=self.derg("alpha"),
                                            in_=self.derg("w"), func=AF.Exp),
               waits=[self.ms("dv")], inc="AC")
        self.E("ac", lambda e: e.activation(out=self.derg("etf"),
                                            in_=self.hfx("chp", G, G),
                                            func=AF.Exp), inc="AC")
        self.drain("ac")
        self.DERdone = self.ms("ac")
        self.E("dv", lambda e: e.memset(self.hfx("small", 0, 64), 0.0),
               waits=[LC], inc="DV")
        self.E("dv", lambda e: e.memset(self.hfx("eps", 0, 1), LNEPS), inc="DV")
        self.E("dv", lambda e: e.tensor_copy(out=self.hbx("onesb", 0, 1),
                                             in_=self.hfx("onesc", 0, 1)), inc="DV")
        self.drain("dv")

    def ln_stats(self, src3, bfc, sqc, rows, bs, src_ready, sq_war=None):
        G_ = G
        m_row, rs_row, t1, t2 = rows
        w0 = [src_ready] + ([sq_war] if sq_war else [])
        bfc3 = bfc.rearrange("p (g t) -> p g t", g=G_)
        sqc3 = sqc.rearrange("p (g t) -> p g t", g=G_)
        self.E("ac", lambda e: e.copy(out=bfc3, in_=src3), waits=w0)
        self.E("ac", lambda e: e.activation(out=sqc3, in_=src3, func=AF.Square),
               inc="AC")
        acv = self.ms("ac")
        onesb = self.hbx("onesb", 0, 1)
        w1 = [acv, self.war.pop(f"ps{bs}", (None, 0)),
              self.war.pop(f"ps{bs + 1}", (None, 0))]
        for gi in range(G_):
            self.E("pe", lambda e, r=bfc.rearrange("p (g t) -> p g t", g=G_)[:, gi, :],
                   st=(gi == 0), sp=(gi == G_ - 1):
                   e.matmul(self.psap[bs][0:1, :], onesb, r, start=st, stop=sp),
                   waits=w1 if gi == 0 else [])
        for gi in range(G_):
            self.E("pe", lambda e, r=sqc.rearrange("p (g t) -> p g t", g=G_)[:, gi, :],
                   st=(gi == 0), sp=(gi == G_ - 1):
                   e.matmul(self.psap[bs + 1][0:1, :], onesb, r, start=st, stop=sp),
                   inc="PE" if gi == G_ - 1 else None)
        pev = self.ms("pe")
        self.E("dv", lambda e: e.tensor_scalar_mul(out=m_row,
                                                   in0=self.psap[bs][0:1, :],
                                                   scalar1=1.0 / D),
               waits=[pev])
        self.drain("dv")
        self.E("dv", lambda e: e.tensor_mul(out=t1, in0=m_row, in1=m_row))
        self.drain("dv")
        self.E("dv", lambda e: e.scalar_tensor_tensor(
            out=t2, in0=self.psap[bs + 1][0:1, :], scalar=1.0 / D, in1=t1,
            op0=ALU.mult, op1=ALU.subtract), inc="DV")
        self.drain("dv")
        self.E("ac", lambda e: e.activation(out=t1, in_=t2, func=AF.Ln,
                                            bias=self.hfx("eps", 0, 1, 0, 1)),
               waits=[self.ms("dv")])
        self.drain("ac")
        self.E("ac", lambda e: e.activation(out=rs_row, in_=t1, func=AF.Exp,
                                            scale=-0.5), inc="AC")
        self.drain("ac")
        acv2 = self.ms("ac")
        self.E("pe", lambda e: e.matmul(self.psap[bs], self.hfx("onesr", 0, 128, 0, 1),
                                        m_row, start=True, stop=True), waits=[acv2])
        self.E("pe", lambda e: e.matmul(self.psap[bs + 1],
                                        self.hfx("onesr", 0, 128, 0, 1),
                                        rs_row, start=True, stop=True), inc="PE")
        return self.ms("pe")

    def rows_pa(self):
        return (self.hfx("stats", 0, TT, 0, 1), self.hfx("stats", TT, TT, 0, 1),
                self.hfx("stats", 0, TT, 32, 1), self.hfx("stats", TT, TT, 32, 1))

    def rows_pb(self):
        return (self.hfx("stats2", 0, TT, 0, 1), self.hfx("stats2", TT, TT, 0, 1),
                self.hfx("stats2", 0, TT, 32, 1), self.hfx("stats2", TT, TT, 32, 1))

    def gbuf(self, og, n=TT, c0=0):
        if og < 4:
            return self.hfx("p0in", og * TT + c0, n)
        if og < 6:
            return self.hfx("p0out", (og - 4) * TT + c0, n)
        return self.hfx("cwb", (og - 6) * TT + c0, n)

    def kkb(self, hc, n=TT, c0=0):
        if hc < 16:
            return self.hbx("uA", hc * TT + c0, n)
        return self.hbx("projbf", 4096 + (hc - 16) * TT + c0, n)

    def gen_pa_head(self, t):
        E, hfx, ps = self.E, self.hfx, self.psap
        xfm_war = self.war.pop("xfm", None)
        for a in range(4):
            half = a % 2
            lx = f"LX{half}"
            w = [self.war.pop(f"xtm{half}", (None, 0))]
            E("sy", lambda e, r0=t * TT + a * 128, d=hfx("xtm", half * 1024, 1024):
              e.dma_start(out=d, in_=self.xe[r0:r0 + 128, :]),
              waits=w, inc=lx, inc_k=16)
            ld = (lx, self.cnt[lx])
            for gi in range(G):
                bank = gi // 4
                w = []
                if gi == 0:
                    w = [ld, self.war.pop("ps0", (None, 0)),
                         self.war.pop("ps1", (None, 0))]
                    if a == 0 and t == 0:
                        w.append(("LC", self.LCv))
                E("pe", lambda e, i=hfx("xtm", half * 1024 + gi * 128, 128),
                  o=ps[bank][:, (gi % 4) * 128:(gi % 4) * 128 + 128]:
                  e.transpose(o, i, self.hfx("ident", 0, 128)),
                  waits=w, inc="PE" if gi == G - 1 else None)
            pev = self.ms("pe")
            self.war[f"xtm{half}"] = pev
            for bi in range(2):
                w = [pev] if bi == 0 else []
                if bi == 0 and a == 0 and xfm_war:
                    w.append(xfm_war)
                out3 = hfx("xfm", 0, G * TT).rearrange(
                    "p (g t) -> p g t", g=G)[:, 4 * bi:4 * bi + 4, a * 128:a * 128 + 128]
                E("dv", lambda e, i=ps[bi][:, 0:512].rearrange("p (g t) -> p g t", g=4),
                  o=out3: e.tensor_copy(out=o, in_=i),
                  waits=w, inc="DV" if bi == 1 else None)
            dvv = self.ms("dv")
            self.war["ps0"] = dvv
            self.war["ps1"] = dvv
        xfm_ms = self.ms("dv")
        bc_outer = self.ln_stats(self.ar3("xfm"), self.hbx("dA", 0, 4096),
                                 self.hbx("dA", 4096, 4096), self.rows_pa(), 4,
                                 xfm_ms, sq_war=self.war.pop("dA", None))
        is_w = t < WM_TILES
        last_w = t == WM_TILES - 1
        mt = t - WM_TILES
        ar = self.ar

        def xap3(c0, n):
            a = hfx("xa", 0, G * TT)
            return a.rearrange("p (g t) -> p g t", g=G)[:, :, c0:c0 + n]

        sm8 = lambda c0: hfx("small", c0, G).unsqueeze(2)
        xn_war = self.war.pop("xn", None)
        E("dv", lambda e: e.tensor_tensor(
            out=self.ar3("xn"), in0=self.ar3("xfm"),
            in1=self.bc_row(ps[4][:, 0:TT]), op=ALU.subtract),
          waits=[bc_outer] + ([xn_war] if xn_war else []))
        E("dv", lambda e: e.tensor_tensor(
            out=self.ar3("xn"), in0=self.ar3("xn"),
            in1=self.bc_row(ps[5][:, 0:TT]), op=ALU.mult), inc="DV")
        xn_ms = self.ms("dv")
        self.war["ps4"] = xn_ms
        self.war["ps5"] = xn_ms

        if not is_w:
            out_ap = self.xn_sc.rearrange("(g p) t -> p g t", p=128)[
                :, :, mt * TT:(mt + 1) * TT]
            E("sy", lambda e, o=out_ap,
              i=hfx("xn", 0, G * TT).rearrange("p (g t) -> p g t", g=G):
              e.dma_start(out=o, in_=i), waits=[xn_ms], inc="SXN", inc_k=16)
            xn_spill = ("SXN", self.cnt["SXN"])
        else:
            xn_spill = None

        bc = self.ln_stats(self.ar3("xn"), self.hbx("dA", 0, 4096),
                           self.hbx("dA", 4096, 4096), self.rows_pa(), 4,
                           xn_ms, sq_war=None)
        xa_war = self.war.pop("xa", None)
        if t == 0:
            E("dv", lambda e: e.memset(self.xa3(0, 1), 0.0))
        else:
            E("dv", lambda e: e.tensor_copy(out=self.xa3(0, 1), in_=sm8(0)),
              waits=[xa_war] if xa_war else [])
        if t == WM_TILES:
            self.drain("dv")
            E("dv", lambda e: e.tensor_scalar_mul(
                out=self.xa3(0, 1), in0=self.xa3(0, 1),
                scalar1=self.hfx("hmask", 0, 1)))
            E("dv", lambda e: e.tensor_scalar_mul(
                out=hfx("small", 16, G), in0=hfx("small", 16, G),
                scalar1=self.hfx("hmask", 0, 1)))
            E("dv", lambda e: e.tensor_scalar_mul(
                out=hfx("small", 24, G), in0=hfx("small", 24, G),
                scalar1=self.hfx("hmask", 0, 1)))
        self.drain("dv")
        E("dv", lambda e: e.tensor_tensor(
            out=self.xa3(1, TT), in0=self.ar3("xn"),
            in1=self.bc_row(ps[4][:, 0:TT]), op=ALU.subtract), waits=[bc])
        E("dv", lambda e: e.tensor_tensor(
            out=self.xa3(1, TT), in0=self.xa3(1, TT),
            in1=self.bc_row(ps[5][:, 0:TT]), op=ALU.mult))
        E("dv", lambda e: e.tensor_tensor(
            out=self.xa3(1, TT), in0=self.xa3(1, TT),
            in1=self.bc_grp(self.chps("g1")), op=ALU.mult))
        E("dv", lambda e: e.tensor_tensor(
            out=self.xa3(1, TT), in0=self.xa3(1, TT),
            in1=self.bc_grp(self.chps("b1")), op=ALU.add), inc="DV")
        xa_ms = self.ms("dv")
        self.war["ps4"] = xa_ms
        self.war["ps5"] = xa_ms
        E("dv", lambda e: e.tensor_copy(out=sm8(0), in_=self.xa3(TT, 1)))
        self.drain("dv")

        return (xn_spill, bc)

    def gen_pa_tail(self, t, head_out):
        is_w = t < WM_TILES
        last_w = t == WM_TILES - 1
        mt = t - WM_TILES
        xn_spill, bc = head_out
        E, ar, hfx, ps = self.E, self.ar, self.hfx, self.psap

        def xap3(c0, n):
            a = hfx("xa", 0, G * TT)
            return a.rearrange("p (g t) -> p g t", g=G)[:, :, c0:c0 + n]

        sm8 = lambda c0: hfx("small", c0, G).unsqueeze(2)

        pj_war = self.war.pop("projbf", None)
        E("dv", lambda e: e.tensor_tensor(
            out=self.ar3("dA"), in0=self.xa3(1, TT), in1=self.xa3(0, TT),
            op=ALU.subtract), waits=[bc])
        self.drain("dv")
        for gi in range(G):
            w = [pj_war] if (gi == 0 and pj_war) else []
            E("dv", lambda e, d_=ar("dA", gi), x0=self.xav(gi, 0, TT),
              o=self.pbf(0, gi), s1=self.chpc("tmk", gi):
              e.scalar_tensor_tensor(out=o, in0=d_, scalar=s1, in1=x0,
                                     op0=ALU.mult, op1=ALU.add), waits=w)
        pj1_war = self.war.pop("projbf1", None)
        for gi in range(G):
            w = [pj1_war] if (gi == 0 and pj1_war) else []
            E("dv", lambda e, d_=ar("dA", gi), x0=self.xav(gi, 0, TT),
              o=self.pbf(1, gi), s1=self.chpc("tmv", gi):
              e.scalar_tensor_tensor(out=o, in0=d_, scalar=s1, in1=x0,
                                     op0=ALU.mult, op1=ALU.add),
              waits=w, inc="DV" if gi == G - 1 else None)
        if (not is_w) or last_w:
            rw_war = self.war.pop("rwkvb", None)
            for gi in range(G):
                w = [rw_war] if (gi == 0 and rw_war) else []
                E("dv", lambda e, d_=ar("dA", gi), x0=self.xav(gi, 0, TT),
                  o=self.rwb(gi), s1=self.chpc("tmr", gi):
                  e.scalar_tensor_tensor(out=o, in0=d_, scalar=s1, in1=x0,
                                         op0=ALU.mult, op1=ALU.add),
                  waits=w, inc="DV" if gi == G - 1 else None)
        mix_ms = self.ms("dv")


        q_war = self.war.pop("qA", None)
        u_war = self.war.pop("uA", None)
        qrd = {}
        for proj in range(2):
            reg = "att_k" if proj == 0 else "att_v"
            for og in range(G):
                lw = self.ring_load(og, self.RG[reg] + og * G, G,
                                    extra=([self.p0_done("P0att")]
                                           if t == 0 and proj == 0 else []))
                bank = 2 + og % 2
                w0 = [lw, mix_ms, self.war.pop(f"ps{bank}", (None, 0))]
                for kg in range(G):
                    E("pe", lambda e, l=self.slot_blk(og, kg), r=self.pbf(proj, kg),
                      b=bank, st=(kg == 0), sp=(kg == G - 1):
                      e.matmul(ps[b], l, r, start=st, stop=sp),
                      waits=w0 if kg == 0 else [],
                      inc="PE" if kg == G - 1 else None)
                self.ring_user[og] = self.ms("pe")
                if proj == 0:
                    w = [self.ms("pe")]
                    if og == 0 and q_war:
                        w.append(q_war)
                    E("ac", lambda e, b=bank, o=ar("qA", og):
                      e.activation(out=o, in_=ps[b], func=AF.Exp),
                      waits=w, inc="AC")
                    self.war[f"ps{bank}"] = self.ms("ac")
                    qrd[og] = self.ms("ac")
                else:
                    w = [self.ms("pe"), qrd[og]]
                    if og == 0 and u_war:
                        w.append(u_war)
                    E("dv", lambda e, b=bank, q_=ar("qA", og), o=ar("uA", og):
                      e.tensor_tensor(out=o, in0=q_, in1=ps[b], op=ALU.mult),
                      waits=w, inc="DV")
                    self.war[f"ps{bank}"] = self.ms("dv")
        qu_ms = self.ms("dv")

        for gi in range(G):
            w = [qu_ms] if gi == 0 else []
            if t == 0 and gi == 0:
                w.append(self.DERdone)
            ini_n = 0.0 if t == 0 else hfx("small", 16 + gi, 1)
            ini_d = 0.0 if t == 0 else hfx("small", 24 + gi, 1)
            E("dv", lambda e, al_=self.derc("alpha", gi).broadcast_to([128, TT]),
              u_=ar("uA", gi), o=ar("xfm", gi), ini=ini_n:
              e.tensor_tensor_scan(out=o, data0=al_, data1=u_, initial=ini,
                                   op0=ALU.mult, op1=ALU.add), waits=w)
            E("dv", lambda e, al_=self.derc("alpha", gi).broadcast_to([128, TT]),
              q_=ar("qA", gi), o=self.xap(gi), ini=ini_d:
              e.tensor_tensor_scan(out=o, data0=al_, data1=q_, initial=ini,
                                   op0=ALU.mult, op1=ALU.add),
              inc="DV" if gi == G - 1 else None)
        self.drain("dv")

        etfb = self.bc_grp(self.derg("etf"), TT - 1)
        etfb1 = self.bc_grp(self.derg("etf"), 1)
        E("dv", lambda e: e.tensor_tensor(
            out=self.ar3("dA", 1, TT - 1), in0=self.ar3("uA", 1, TT - 1),
            in1=etfb, op=ALU.mult))
        E("dv", lambda e: e.tensor_tensor(
            out=self.ar3("dA", 1, TT - 1), in0=self.ar3("dA", 1, TT - 1),
            in1=self.ar3("xfm", 0, TT - 1), op=ALU.add))
        E("dv", lambda e: e.tensor_tensor(
            out=self.ar3("dA", 0, 1), in0=self.ar3("uA", 0, 1),
            in1=etfb1, op=ALU.mult))
        self.drain("dv")
        if t > 0:
            E("dv", lambda e: e.tensor_tensor(
                out=self.ar3("dA", 0, 1), in0=self.ar3("dA", 0, 1),
                in1=sm8(16), op=ALU.add))
            self.drain("dv")
        E("dv", lambda e: e.tensor_tensor(
            out=self.ar3("uA", 1, TT - 1), in0=self.ar3("qA", 1, TT - 1),
            in1=etfb, op=ALU.mult))
        E("dv", lambda e: e.tensor_tensor(
            out=self.ar3("uA", 1, TT - 1), in0=self.ar3("uA", 1, TT - 1),
            in1=xap3(0, TT - 1), op=ALU.add))
        E("dv", lambda e: e.tensor_tensor(
            out=self.ar3("uA", 0, 1), in0=self.ar3("qA", 0, 1),
            in1=etfb1, op=ALU.mult))
        self.drain("dv")
        if t > 0:
            E("dv", lambda e: e.tensor_tensor(
                out=self.ar3("uA", 0, 1), in0=self.ar3("uA", 0, 1),
                in1=sm8(24), op=ALU.add))
            self.drain("dv")
        E("dv", lambda e: e.tensor_copy(out=sm8(16), in_=self.ar3("xfm", TT - 1, 1)))
        E("dv", lambda e: e.tensor_copy(out=sm8(24), in_=xap3(TT - 1, 1)), inc="DV")
        self.drain("dv")
        n0_ms = self.ms("dv")

        if is_w and not last_w:
            for k in ["dA", "qA", "uA", "xfm", "xn"]:
                self.war[k] = n0_ms
            self.war["projbf"] = self.ms("pe")
            self.war["projbf1"] = self.ms("pe")
            self.pump_p0(88)
            return None

        for og in range(G):
            lw = self.ring_load(og, self.RG["att_r"] + og * G, G)
            bank = 2 + og % 2
            w0 = [lw, self.war.pop(f"ps{bank}", (None, 0))]
            if og == 0:
                w0.append(n0_ms)
            if last_w:
                for kg in range(G):
                    E("pe", lambda e, l=self.slot_blk(og, kg),
                      r=self.rwb(kg, 1, TT - 1), b=bank,
                      st=(kg == 0), sp=(kg == G - 1):
                      e.matmul(ps[b][:, 0:1], l, r, start=st, stop=sp),
                      waits=w0 if kg == 0 else [],
                      inc="PE" if kg == G - 1 else None)
                self.ring_user[og] = self.ms("pe")
                E("ac", lambda e, b=bank, o=hfx("small", 48 + og, 1):
                  e.activation(out=o, in_=ps[b][:, 0:1], func=AF.Sigmoid),
                  waits=[self.ms("pe")], inc="AC")
            else:
                for kg in range(G):
                    E("pe", lambda e, l=self.slot_blk(og, kg), r=self.rwb(kg),
                      b=bank, st=(kg == 0), sp=(kg == G - 1):
                      e.matmul(ps[b], l, r, start=st, stop=sp),
                      waits=w0 if kg == 0 else [],
                      inc="PE" if kg == G - 1 else None)
                self.ring_user[og] = self.ms("pe")
                E("ac", lambda e, b=bank, o=self.xap(og):
                  e.activation(out=o, in_=ps[b], func=AF.Sigmoid),
                  waits=[self.ms("pe")], inc="AC")
            self.war[f"ps{bank}"] = self.ms("ac")
        sr_ms = self.ms("ac")

        if last_w:
            for gi in range(G):
                E("dv", lambda e, i=ar("uA", gi, 1, TT - 1), o=hfx("t2", gi, 1):
                  e.reciprocal(out=o, in_=i), waits=[sr_ms] if gi == 0 else [])
            self.drain("dv")
            for gi in range(G):
                E("dv", lambda e, a=ar("dA", gi, 1, TT - 1), b=hfx("t2", gi, 1),
                  o=hfx("small", 32 + gi, 1):
                  e.tensor_tensor(out=o, in0=a, in1=b, op=ALU.mult))
            self.drain("dv")
            for gi in range(G):
                E("dv", lambda e, o=hfx("small", 32 + gi, 1),
                  s=hfx("small", 48 + gi, 1):
                  e.tensor_scalar_mul(out=o, in0=o, scalar1=s))
            self.drain("dv")
            E("dv", lambda e: e.tensor_copy(out=self.hbx("rwhb", 0, 8),
                                            in_=hfx("small", 32, G)))
            self.drain("dv")
            E("dv", lambda e: e.tensor_copy(out=sm8(40),
                                            in_=self.ar3("xn", TT - 1, 1)),
              inc="DV")
            self.drain("dv")
            hal = self.ms("dv")
            for k in ["dA", "qA", "uA", "xfm", "xn", "xa"]:
                self.war[k] = hal
            self.war["projbf"] = self.ms("pe")
            self.war["projbf1"] = self.ms("pe")
            self.pump_p0(88)
            return None

        E("ac", lambda e: e.activation(out=hfx("qA", 0, G * TT),
                                       in_=hfx("uA", 0, G * TT), func=AF.Ln),
          waits=[n0_ms])
        E("ac", lambda e: e.activation(out=hfx("xfm", 0, G * TT),
                                       in_=hfx("qA", 0, G * TT), func=AF.Exp,
                                       scale=-1.0), inc="AC")
        rec_ms = self.ms("ac")
        self.war["qA"] = rec_ms
        self.war["uA"] = rec_ms
        E("dv", lambda e: e.tensor_tensor(out=hfx("dA", 0, G * TT),
                                          in0=hfx("dA", 0, G * TT),
                                          in1=hfx("xfm", 0, G * TT), op=ALU.mult),
          waits=[rec_ms])
        E("dv", lambda e: e.tensor_tensor(out=self.hbx("rwkvb", 0, 4096),
                                          in0=hfx("dA", 0, G * TT),
                                          in1=hfx("xa", 0, G * TT), op=ALU.mult),
          waits=[sr_ms], inc="DV")
        rw_ms = self.ms("dv")
        self.war["dA"] = rw_ms
        self.war["xfm"] = rw_ms
        self.pump_p0(8)
        return (mt, xn_spill, rw_ms)

    def gen_pb_a(self, pend):
        mt, xn_spill, rw_ms = pend
        E, ar, hfx, ps = self.E, self.ar, self.hfx, self.psap
        sm8 = lambda c0: hfx("small", c0, G).unsqueeze(2)

        xmid_war = self.war.pop("xmid", None)
        for cg in range(G):
            lw = self.ring_load(cg, self.RG["wo"] + cg * G, G,
                                extra=([self.p0_done("P0wo")] if mt == 0 else []))
            bank = 6 + cg % 2
            w0 = [lw, self.war.pop(f"ps{bank}", (None, 0))]
            if cg == 0:
                w0.append(rw_ms)
            for ko in range(G):
                E("pe", lambda e, l=self.slot_blk(cg, ko), r=self.rwb(ko),
                  b=bank, st=(ko == 0), sp=(ko == G - 1):
                  e.matmul(ps[b], l, r, start=st, stop=sp),
                  waits=w0 if ko == 0 else [],
                  inc="PE" if ko == G - 1 else None)
            self.ring_user[cg] = self.ms("pe")
            w = [self.ms("pe")]
            if cg == 0 and xmid_war:
                w.append(xmid_war)
            E("dv", lambda e, b=bank, i=ar("xn", cg), o=ar("xmid", cg):
              e.tensor_tensor(out=o, in0=i, in1=ps[b], op=ALU.add),
              waits=w, inc="DV")
            self.war[f"ps{bank}"] = self.ms("dv")
        xmid_ms = self.ms("dv")
        self.war["rwkvb"] = self.ms("pe")
        wo_read_xn = self.ms("dv")
        self.war["xn"] = (xn_spill[0], xn_spill[1]) if xn_spill else wo_read_xn

        if mt == 0:
            w0 = [self.war.pop("ps2", (None, 0)), self.war.pop("ps3", (None, 0)),
                  self.ms("dv")]
            for cg in range(G):
                lw = self.ring_load(cg, self.RG["wo"] + cg * G, G)
                for ko in range(G):
                    E("pe", lambda e, l=self.slot_blk(cg, ko),
                      r=self.hbx("rwhb", ko, 1), b=2, c=cg,
                      st=(ko == 0), sp=(ko == G - 1):
                      e.matmul(ps[b][:, c:c + 1], l, r, start=st, stop=sp),
                      waits=(w0 + [lw]) if ko == 0 else [],
                      inc="PE" if ko == G - 1 else None)
                self.ring_user[cg] = self.ms("pe")
            pev = self.ms("pe")
            t2 = lambda c0, n=G: hfx("t2", c0, n)
            E("dv", lambda e: e.tensor_tensor(out=t2(8), in0=hfx("small", 40, G),
                                              in1=ps[2][:, 0:G], op=ALU.add),
              waits=[pev], inc="DV")
            self.drain("dv")
            xmh_dv = self.ms("dv")
            E("ac", lambda e: e.activation(out=t2(16), in_=t2(8), func=AF.Square),
              waits=[xmh_dv], inc="AC")
            self.drain("ac")
            acv = self.ms("ac")
            for cg in range(G):
                E("pe", lambda e, l=t2(8 + cg, 1), c=cg:
                  e.matmul(ps[3][0:1, c:c + 1], l, self.hfx("onesc", 0, 1),
                           start=True, stop=True),
                  waits=[xmh_dv, acv] if cg == 0 else [])
            for cg in range(G):
                E("pe", lambda e, l=t2(16 + cg, 1), c=cg:
                  e.matmul(ps[3][0:1, G + c:G + c + 1], l, self.hfx("onesc", 0, 1),
                           start=True, stop=True),
                  inc="PE" if cg == G - 1 else None)
            pev = self.ms("pe")
            st_m = hfx("stats2", 0, 1, 0, 1)
            st_v = hfx("stats2", 1, 1, 0, 1)
            st_r = hfx("stats2", 2, 1, 0, 1)
            E("dv", lambda e: e.reduce_sum(out=st_m, in_=ps[3][0:1, 0:G], axis=AX),
              waits=[pev])
            self.drain("dv")
            E("dv", lambda e: e.reduce_sum(out=st_v, in_=ps[3][0:1, G:2 * G], axis=AX))
            self.drain("dv")
            E("dv", lambda e: e.tensor_scalar_mul(out=st_m, in0=st_m, scalar1=1.0 / D))
            self.drain("dv")
            E("dv", lambda e: e.tensor_mul(out=st_r, in0=st_m, in1=st_m))
            self.drain("dv")
            E("dv", lambda e: e.scalar_tensor_tensor(out=st_v, in0=st_v,
                                                     scalar=1.0 / D, in1=st_r,
                                                     op0=ALU.mult, op1=ALU.subtract),
              inc="DV")
            self.drain("dv")
            dvv = self.ms("dv")
            E("ac", lambda e: e.activation(out=st_v, in_=st_v, func=AF.Ln,
                                           bias=self.hfx("eps", 0, 1, 0, 1)),
              waits=[dvv])
            self.drain("ac")
            E("ac", lambda e: e.activation(out=st_v, in_=st_v, func=AF.Exp,
                                           scale=-0.5), inc="AC")
            self.drain("ac")
            acv = self.ms("ac")
            E("pe", lambda e: e.matmul(ps[3][:, 16:17],
                                       self.hfx("onesr", 0, 128, 0, 1), st_m,
                                       start=True, stop=True), waits=[acv])
            E("pe", lambda e: e.matmul(ps[3][:, 17:18],
                                       self.hfx("onesr", 0, 128, 0, 1), st_v,
                                       start=True, stop=True), inc="PE")
            pev = self.ms("pe")
            E("dv", lambda e: e.tensor_tensor(out=t2(8), in0=t2(8),
                                              in1=ps[3][:, 16:17].broadcast_to([128, G]),
                                              op=ALU.subtract), waits=[pev])
            self.drain("dv")
            E("dv", lambda e: e.tensor_tensor(out=t2(8), in0=t2(8),
                                              in1=ps[3][:, 17:18].broadcast_to([128, G]),
                                              op=ALU.mult))
            self.drain("dv")
            E("dv", lambda e: e.tensor_tensor(out=t2(8), in0=t2(8),
                                              in1=self.chps("g2"), op=ALU.mult))
            self.drain("dv")
            E("dv", lambda e: e.tensor_tensor(out=t2(8), in0=t2(8),
                                              in1=self.chps("b2"), op=ALU.add))
            self.drain("dv")
            E("dv", lambda e: e.tensor_scalar_mul(out=t2(8), in0=t2(8),
                                                  scalar1=self.hfx("hmask", 0, 1)))
            self.drain("dv")
            E("dv", lambda e: e.tensor_copy(out=hfx("small", 8, G), in_=t2(8)),
              inc="DV")
            self.drain("dv")
            self.war["ps2"] = self.ms("dv")
            self.war["ps3"] = self.ms("dv")

        bc = self.ln_stats(self.ar3("xmid"), self.hbx("uA", 0, 4096),
                           self.hbx("uA", 4096, 4096), self.rows_pb(), 2,
                           xmid_ms, sq_war=self.war.pop("uA", None))
        xa_war = self.war.pop("xa", None)
        E("dv", lambda e: e.tensor_copy(out=self.xa3(0, 1), in_=sm8(8)),
          waits=[xa_war] if xa_war else [])
        self.drain("dv")
        E("dv", lambda e: e.tensor_tensor(
            out=self.xa3(1, TT), in0=self.ar3("xmid"),
            in1=self.bc_row(ps[2][:, 0:TT]), op=ALU.subtract), waits=[bc])
        E("dv", lambda e: e.tensor_tensor(
            out=self.xa3(1, TT), in0=self.xa3(1, TT),
            in1=self.bc_row(ps[3][:, 0:TT]), op=ALU.mult))
        E("dv", lambda e: e.tensor_tensor(
            out=self.xa3(1, TT), in0=self.xa3(1, TT),
            in1=self.bc_grp(self.chps("g2")), op=ALU.mult))
        E("dv", lambda e: e.tensor_tensor(
            out=self.xa3(1, TT), in0=self.xa3(1, TT),
            in1=self.bc_grp(self.chps("b2")), op=ALU.add), inc="DV")
        self.war["ps2"] = self.ms("dv")
        self.war["ps3"] = self.ms("dv")
        E("dv", lambda e: e.tensor_copy(out=sm8(8), in_=self.xa3(TT, 1)))
        self.drain("dv")

        pj_war = self.war.pop("projbf", None)
        E("dv", lambda e: e.tensor_tensor(
            out=self.ar3("uA"), in0=self.xa3(1, TT), in1=self.xa3(0, TT),
            op=ALU.subtract), waits=[bc])
        self.drain("dv")
        for gi in range(G):
            w = [pj_war] if (gi == 0 and pj_war) else []
            E("dv", lambda e, d_=ar("uA", gi), x0=self.xav(gi, 0, TT),
              o=self.pbf(0, gi), s1=self.chpc("cmk", gi):
              e.scalar_tensor_tensor(out=o, in0=d_, scalar=s1, in1=x0,
                                     op0=ALU.mult, op1=ALU.add), waits=w)
        rw_war = self.war.pop("rwkvb", None)
        for gi in range(G):
            w = [rw_war] if (gi == 0 and rw_war) else []
            E("dv", lambda e, d_=ar("uA", gi), x0=self.xav(gi, 0, TT),
              o=self.rwb(gi), s1=self.chpc("cmr", gi):
              e.scalar_tensor_tensor(out=o, in0=d_, scalar=s1, in1=x0,
                                     op0=ALU.mult, op1=ALU.add),
              waits=w, inc="DV" if gi == G - 1 else None)
        mix2_ms = self.ms("dv")
        self.war["xa"] = mix2_ms

        q_war = self.war.pop("gbuf", None)
        for og in range(G):
            lw = self.ring_load(og, self.RG["fwr"] + og * G, G)
            bank = 6 + og % 2
            w0 = [lw, mix2_ms, self.war.pop(f"ps{bank}", (None, 0))]
            for kg in range(G):
                E("pe", lambda e, l=self.slot_blk(og, kg), r=self.rwb(kg),
                  b=bank, st=(kg == 0), sp=(kg == G - 1):
                  e.matmul(ps[b], l, r, start=st, stop=sp),
                  waits=w0 if kg == 0 else [],
                  inc="PE" if kg == G - 1 else None)
            self.ring_user[og] = self.ms("pe")
            w = [self.ms("pe")]
            if og == 0 and q_war:
                w.append(q_war)
            E("ac", lambda e, b=bank, o=self.gbuf(og):
              e.activation(out=o, in_=ps[b], func=AF.Sigmoid),
              waits=w, inc="AC")
            self.war[f"ps{bank}"] = self.ms("ac")
        gate_ms = self.ms("ac")
        self.war["rwkvb"] = self.ms("pe")
        return (mt, mix2_ms, gate_ms)

    def gen_pb_b(self, pend, stA):
        mt, xn_spill, rw_ms = pend
        _, mix2_ms, gate_ms = stA
        E, ar, hfx, ps = self.E, self.ar, self.hfx, self.psap

        for hc in range(HFG):
            si = hc % 8
            lw = self.ring_load(si, self.RG["fwk"] + hc * G, G,
                                extra=([self.p0_done("P0ffn")]
                                       if mt == 0 and hc == 0 else []))
            bank = 6 + hc % 2
            w0 = [lw, self.war.pop(f"ps{bank}", (None, 0))]
            if hc == 0:
                w0.append(mix2_ms)
            for kg in range(G):
                E("pe", lambda e, l=self.slot_blk(si, kg), r=self.pbf(0, kg),
                  b=bank, st=(kg == 0), sp=(kg == G - 1):
                  e.matmul(ps[b], l, r, start=st, stop=sp),
                  waits=w0 if kg == 0 else [],
                  inc="PE" if kg == G - 1 else None)
            self.ring_user[si] = self.ms("pe")
            sc = hc % 2
            w = [self.ms("pe"), self.war.pop(f"scr{sc}", (None, 0))]
            if hc == 16:
                w.append(self.war.pop("projbf1", (None, 0)))
            E("ac", lambda e, b=bank, o=hfx("scr", sc * TT, TT):
              e.activation(out=o, in_=ps[b], func=AF.Relu), waits=w, inc="AC")
            self.war[f"ps{bank}"] = self.ms("ac")
            E("pl", lambda e, i=hfx("scr", sc * TT, TT), o=self.kkb(hc):
              e.tensor_tensor(out=o, in0=i, in1=i, op=ALU.mult),
              waits=[self.ms("ac")], inc="PL")
            self.war[f"scr{sc}"] = self.ms("pl")
        kk_ms = self.ms("pl")
        self.war["projbf"] = self.ms("pe")

        for og in range(G):
            hs = 8 + og % 2
            lw0 = self.ring_load(hs, self.RG["fwv"] + og * HFG, 16)
            bank = 6 + og % 2
            w0 = [lw0, kk_ms, gate_ms, self.war.pop(f"ps{bank}", (None, 0))]
            for hk in range(HFG):
                if hk == 16:
                    lw1 = self.ring_load(hs, self.RG["fwv"] + og * HFG + 16, 16)
                    w1 = [lw1]
                else:
                    w1 = []
                E("pe", lambda e, l=self.slot_blk(hs, hk % 16), r=self.kkb(hk),
                  b=bank, st=(hk == 0), sp=(hk == HFG - 1):
                  e.matmul(ps[b], l, r, start=st, stop=sp),
                  waits=(w0 if hk == 0 else w1),
                  inc="PE" if hk in (15, HFG - 1) else None)
                if hk == 15:
                    self.ring_user[hs] = self.ms("pe")
            self.ring_user[hs] = self.ms("pe")
            pev = self.ms("pe")
            E("ac", lambda e, b=bank, o=hfx("scr2", 0, TT):
              e.copy(out=o, in_=ps[b]),
              waits=[pev, self.war.pop("scr2", (None, 0))], inc="AC")
            acv = self.ms("ac")
            self.war[f"ps{bank}"] = acv
            E("pl", lambda e, g_=self.gbuf(og), o=hfx("scr2", 0, TT):
              e.tensor_tensor(out=o, in0=o, in1=g_, op=ALU.mult),
              waits=([acv, gate_ms] if og == 0 else [acv]))
            self.drain("pl")
            E("pl", lambda e, o=ar("xmid", og), i=hfx("scr2", 0, TT):
              e.tensor_tensor(out=o, in0=o, in1=i, op=ALU.add), inc="PL")
            self.drain("pl")
            self.war["scr2"] = self.ms("pl")
        s_ms = self.ms("pl")
        out_ap = self.s_sc.rearrange("(g p) t -> p g t", p=128)[
            :, :, mt * TT:(mt + 1) * TT]
        E("sy", lambda e, o=out_ap,
          i=hfx("xmid", 0, G * TT).rearrange("p (g t) -> p g t", g=G):
          e.dma_start(out=o, in_=i), waits=[s_ms], inc="SS", inc_k=16)

        self.war["xmid"] = ("SS", self.cnt["SS"])
        self.war["gbuf"] = s_ms
        self.war["uA"] = self.ms("pe")
        self.war["projbf1"] = self.ms("pe")
        self.pump_p0(8)

    def accv(self, og, sub):
        reg = "xfm" if og < 4 else "dA"
        return self.hfx(reg, (og % 4) * 1024 + sub * TT, TT)

    def eob(self, sub, hc, n=TT, c0=0):
        reg = "xn" if sub == 0 else "uA"
        return self.hbx(reg, hc * TT + c0, n)

    def xnbv(self, gi, sub, n=TT, c0=0):
        return self.hbx("qA", gi * 1024 + sub * TT + c0, n)

    def gen_pc_st(self, u):
        E, ar, hfx, ps = self.E, self.ar, self.hfx, self.psap
        G_ = G
        onesr = self.hfx("onesr", 0, 128, 0, 1)
        lgsb = self.hfx("lgsb", 0, TT, 0, 8)

        top2_dv = []
        for sub in range(2):
            mt0 = u * 1024 + sub * TT
            w = [self.war.pop("xn_pc", (None, 0)),
                 ("SXN", self.cnt["SXN"])]
            in_ap = self.xn_sc.rearrange("(g p) t -> p g t", p=128)[
                :, :, mt0:mt0 + TT]
            E("sy", lambda e, i=in_ap,
              o=hfx("xn", 0, G_ * TT).rearrange("p (g t) -> p g t", g=G_):
              e.dma_start(out=o, in_=i), waits=w, inc="RXN", inc_k=16)
            rxn = ("RXN", self.cnt["RXN"])
            w0 = [rxn, self.war.pop("ps4", (None, 0))]
            if u == 0 and sub == 0:
                w0.append(("LC", self.LCv))
            for kg in range(G_):
                E("pe", lambda e, l=hfx("rwt", kg * 8, 8), r=ar("xn", kg),
                  st=(kg == 0), sp=(kg == G_ - 1):
                  e.matmul(ps[4][0:8, :], l, r, start=st, stop=sp),
                  waits=w0 if kg == 0 else [],
                  inc="PE" if kg == G_ - 1 else None)
            pev = self.ms("pe")
            E("dv", lambda e: e.tensor_scalar(
                out=lgsb, in0=ps[4][0:8, :], scalar1=self.hfx("ebias", 0, 1, 0, 8),
                scalar2=None, op0=ALU.add), waits=[pev], inc="DV")
            self.drain("dv")
            lg_ms = self.ms("dv")
            self.war["ps4"] = lg_ms
            for gi in range(G_):
                E("dv", lambda e, i=ar("xn", gi), o=self.xnbv(gi, sub):
                  e.tensor_copy(out=o, in_=i),
                  waits=[pev] if gi == 0 else [],
                  inc="DV" if gi == G_ - 1 else None)
            self.drain("dv")
            self.war["xn_pc"] = self.ms("dv")

            t2c = lambda c0, n: self.hfx("t2", c0, n)
            cwt_ps_w = [self.war.pop("ps5", (None, 0))]
            for c in range(4):
                E("pe", lambda e, i=lgsb[0:8, c * 128:(c + 1) * 128],
                  o=ps[5][:, c * 8:(c + 1) * 8]:
                  e.transpose(o, i, self.hfx("ident", 0, 8, 0, 8)),
                  waits=([lg_ms] + cwt_ps_w) if c == 0 else [],
                  inc="PE" if c == 3 else None)
            pev = self.ms("pe")
            for c in range(4):
                E("dv", lambda e, i=ps[5][:, c * 8:(c + 1) * 8],
                  o=self.hfx("lgT", c * 8, 8): e.tensor_copy(out=o, in_=i),
                  waits=[pev] if c == 0 else [],
                  inc="DV" if c == 3 else None)
            self.drain("dv")
            self.war["ps5"] = self.ms("dv")
            for c in range(4):
                lg = self.hfx("lgT", c * 8, 8)
                cw = self.hfx("cwt", c * 8, 8)
                m1 = t2c(32, 1); m2 = t2c(33, 1); r1 = t2c(34, 1); r2 = t2c(35, 1)
                eq1 = t2c(0, 8); eq2 = t2c(8, 8); l2 = t2c(16, 8)
                E("dv", lambda e, lg=lg, m1=m1: e.reduce_max(out=m1, in_=lg, axis=AX))
                self.drain("dv")
                E("dv", lambda e, lg=lg, m1=m1, eq1=eq1: e.tensor_scalar(
                    out=eq1, in0=lg, scalar1=m1, scalar2=None, op0=ALU.is_equal))
                self.drain("dv")
                E("dv", lambda e, lg=lg, eq1=eq1, l2=l2: e.scalar_tensor_tensor(
                    out=l2, in0=eq1, scalar=-1e9, in1=lg, op0=ALU.mult, op1=ALU.add))
                self.drain("dv")
                E("dv", lambda e, l2=l2, m2=m2: e.reduce_max(out=m2, in_=l2, axis=AX))
                self.drain("dv")
                E("dv", lambda e, l2=l2, m2=m2, eq2=eq2: e.tensor_scalar(
                    out=eq2, in0=l2, scalar1=m2, scalar2=None, op0=ALU.is_equal))
                self.drain("dv")
                E("dv", lambda e, m1=m1, m2=m2, d_=t2c(36, 1): e.tensor_tensor(
                    out=d_, in0=m1, in1=m2, op=ALU.subtract), inc="DV")
                self.drain("dv")
                E("ac", lambda e, d_=t2c(36, 1), r1=r1: e.activation(
                    out=r1, in_=d_, func=AF.Tanh, scale=0.5),
                  waits=[self.ms("dv")], inc="AC")
                self.drain("ac")
                E("dv", lambda e, r1=r1: e.tensor_scalar(
                    out=r1, in0=r1, scalar1=0.5, scalar2=0.5,
                    op0=ALU.mult, op1=ALU.add), waits=[self.ms("ac")])
                self.drain("dv")
                E("dv", lambda e, r1=r1, r2=r2: e.tensor_scalar(
                    out=r2, in0=r1, scalar1=-1.0, scalar2=1.0,
                    op0=ALU.mult, op1=ALU.add))
                self.drain("dv")
                E("dv", lambda e, eq1=eq1, r1=r1, cw=cw: e.tensor_scalar(
                    out=cw, in0=eq1, scalar1=r1, scalar2=None, op0=ALU.mult))
                self.drain("dv")
                E("dv", lambda e, eq2=eq2, r2=r2, l2=l2: e.tensor_scalar(
                    out=l2, in0=eq2, scalar1=r2, scalar2=None, op0=ALU.mult))
                self.drain("dv")
                E("dv", lambda e, cw=cw, l2=l2: e.tensor_tensor(
                    out=cw, in0=cw, in1=l2, op=ALU.add),
                  inc="DV" if c == 3 else None)
                self.drain("dv")
            cw_ms = self.ms("dv")
            w0 = [cw_ms, self.war.pop("ps5", (None, 0))]
            for c in range(4):
                E("pe", lambda e, i=self.hfx("cwt", c * 8, 8),
                  o=ps[5][0:8, c * 128:(c + 1) * 128]:
                  e.transpose(o, i, self.hfx("ident", 0, 128)),
                  waits=w0 if c == 0 else [], inc="PE" if c == 3 else None)
            pev = self.ms("pe")
            cwT = self.hfx("cwrow", 0, TT, 0, 8)
            E("dv", lambda e, o=cwT: e.tensor_copy(out=o, in_=ps[5][0:8, :]),
              waits=[pev], inc="DV")
            self.drain("dv")
            self.war["ps5"] = self.ms("dv")
            E("sy", lambda e, o=self.cw_sc[:, sub * TT:(sub + 1) * TT], i=cwT:
              e.dma_start(out=o, in_=i), waits=[self.ms("dv")],
              inc="SCW", inc_k=16)
            top2_dv.append(("SCW", self.cnt["SCW"]))

        if PC_PARTS < 2:
            return
        for ex in range(PC_NEX):
            shared = ex == NEX - 1
            if not shared:
                for sub in range(2):
                    rsem = f"RCW{sub}"
                    w = [top2_dv[sub], self.war.pop("cwrow", (None, 0))]
                    E("sy", lambda e, i=self.cw_sc[ex:ex + 1, sub * TT:(sub + 1) * TT],
                      o=self.hfx("cwrow", TT, TT, 0, 1):
                      e.dma_start(out=o, in_=i), waits=w, inc=rsem, inc_k=16)
                    rv = (rsem, self.cnt[rsem])
                    w0 = [rv, self.war.pop("ps4", (None, 0))]
                    E("pe", lambda e, r=self.hfx("cwrow", TT, TT, 0, 1):
                      e.matmul(ps[4], onesr, r, start=True, stop=True),
                      waits=w0, inc="PE")
                    pev = self.ms("pe")
                    self.war["cwrow"] = pev
                    E("dv", lambda e, o=self.hfx("cwb", sub * TT, TT):
                      e.tensor_copy(out=o, in_=ps[4]),
                      waits=[pev, self.war.pop(f"cwb{sub}", (None, 0)),
                             self.war.pop("gbuf", (None, 0))], inc="DV")
                    self.drain("dv")
                    self.war["ps4"] = self.ms("dv")
            cwb_ms = self.ms("dv")

            for hc in range(HMG):
                s1 = self.ring_load(hc % 4, self.RG["moe1"] + (ex * HMG + hc) * G_,
                                    G_, extra=([self.p0_done("P0moe")]
                                               if u == 0 and ex == 0 and hc < 4
                                               else []))
                s3 = self.ring_load(4 + hc % 4, self.RG["moe3"] + (ex * HMG + hc) * G_,
                                    G_)
                for sub in range(2):
                    bank_g, bank_u = sub, 2 + sub
                    w0 = [s1, self.war.pop(f"ps{bank_g}", (None, 0))]
                    if hc == 0 and sub == 0:
                        w0.append(cwb_ms)
                        w0.append(self.war.pop("eo", (None, 0)))
                    for kg in range(G_):
                        E("pe", lambda e, l=self.slot_blk(hc % 4, kg),
                          r=self.xnbv(kg, sub), b=bank_g,
                          st=(kg == 0), sp=(kg == G_ - 1):
                          e.matmul(ps[b], l, r, start=st, stop=sp),
                          waits=w0 if kg == 0 else [],
                          inc="PE" if kg == G_ - 1 else None)
                    w0 = [s3, self.war.pop(f"ps{bank_u}", (None, 0))]
                    for kg in range(G_):
                        E("pe", lambda e, l=self.slot_blk(4 + hc % 4, kg),
                          r=self.xnbv(kg, sub), b=bank_u,
                          st=(kg == 0), sp=(kg == G_ - 1):
                          e.matmul(ps[b], l, r, start=st, stop=sp),
                          waits=w0 if kg == 0 else [],
                          inc="PE" if kg == G_ - 1 else None)
                    pev = self.ms("pe")
                    if sub == 1:
                        self.ring_user[hc % 4] = pev
                        self.ring_user[4 + hc % 4] = pev
                    sc = self.hfx("scr", sub * TT, TT)
                    E("ac", lambda e, b=bank_g, o=sc:
                      e.activation(out=o, in_=ps[b], func=AF.Silu),
                      waits=[pev, self.war.pop(f"scr{sub}", (None, 0))], inc="AC")
                    acv = self.ms("ac")
                    self.war[f"ps{bank_g}"] = acv
                    if shared:
                        E("dv", lambda e, b=bank_u, i=sc, o=self.eob(sub, hc):
                          e.tensor_tensor(out=o, in0=i, in1=ps[b], op=ALU.mult),
                          waits=[acv], inc="DV")
                        self.drain("dv")
                    else:
                        sc2 = self.hfx("scr2", 0, TT)
                        E("dv", lambda e, b=bank_u, i=sc, o=sc2:
                          e.tensor_tensor(out=o, in0=i, in1=ps[b], op=ALU.mult),
                          waits=[acv], inc="DV")
                        self.drain("dv")
                        E("dv", lambda e, i=sc2, cwb=self.hfx("cwb", sub * TT, TT),
                          o=self.eob(sub, hc):
                          e.tensor_tensor(out=o, in0=i, in1=cwb, op=ALU.mult),
                          inc="DV")
                        self.drain("dv")
                    dvv = self.ms("dv")
                    self.war[f"ps{bank_u}"] = dvv
                    self.war[f"scr{sub}"] = dvv
            eo_ms = self.ms("dv")
            self.war[f"cwb0"] = eo_ms
            self.war[f"cwb1"] = eo_ms

            for og in range(G_):
                s2s = self.ring_load(8 + og % 2, self.RG["moe2"] + (ex * G_ + og) * HMG,
                                     HMG)
                for sub in range(2):
                    bank = 4 + 2 * (og % 2) + sub
                    w0 = [s2s, eo_ms, self.war.pop(f"ps{bank}", (None, 0))]
                    for hk in range(HMG):
                        E("pe", lambda e, l=self.slot_blk(8 + og % 2, hk),
                          r=self.eob(sub, hk), b=bank,
                          st=(hk == 0), sp=(hk == HMG - 1):
                          e.matmul(ps[b], l, r, start=st, stop=sp),
                          waits=w0 if hk == 0 else [],
                          inc="PE" if hk == HMG - 1 else None)
                    pev = self.ms("pe")
                    if sub == 1:
                        self.ring_user[8 + og % 2] = pev
                    if ex == 0:
                        E("dv", lambda e, b=bank, o=self.accv(og, sub):
                          e.tensor_copy(out=o, in_=ps[b]),
                          waits=[pev] + ([self.war.pop("acc", (None, 0))]
                                         if og == 0 and sub == 0 else []),
                          inc="DV")
                    else:
                        E("dv", lambda e, b=bank, o=self.accv(og, sub):
                          e.tensor_tensor(out=o, in0=o, in1=ps[b], op=ALU.add),
                          waits=[pev], inc="DV")
                    self.drain("dv")
                    self.war[f"ps{bank}"] = self.ms("dv")
            self.war["eo"] = self.ms("pe")
        acc_ms = self.ms("dv")
        if PC_NEX < NEX:
            return

        if PC_PARTS < 3:
            return
        for sub in range(2):
            mt0 = u * 1024 + sub * TT
            w = [("SS", self.cnt["SS"]), self.war.pop("s_pc", (None, 0))]
            in_ap = self.s_sc.rearrange("(g p) t -> p g t", p=128)[
                :, :, mt0:mt0 + TT]
            E("sy", lambda e, i=in_ap,
              o=hfx("projbf", 0, G_ * TT).rearrange("p (g t) -> p g t", g=G_):
              e.dma_start(out=o, in_=i), waits=w, inc="RSS", inc_k=16)
            rss = ("RSS", self.cnt["RSS"])
            for og in range(G_):
                sc = self.hfx("scr", (og % 2) * TT, TT)
                E("ac", lambda e, i=self.accv(og, sub), o=sc:
                  e.activation(out=o, in_=i, func=AF.Silu),
                  waits=[acc_ms, self.war.pop(f"scr{og % 2}", (None, 0))]
                  + ([self.war.pop("xa_pc", (None, 0))] if og == 0 else []),
                  inc="AC")
                acv = self.ms("ac")
                E("dv", lambda e, i=sc, s_=hfx("projbf", og * TT, TT),
                  o=self.xap(og): e.tensor_tensor(out=o, in0=i, in1=s_,
                                                  op=ALU.mult),
                  waits=[acv, rss] if og == 0 else [acv], inc="DV")
                self.drain("dv")
                self.war[f"scr{og % 2}"] = self.ms("dv")
            t_ms = self.ms("dv")

            for a in range(4):
                half = a % 2
                w = [self.war.pop(f"xtm_pc{half}", (None, 0)),
                     self.war.pop(f"xtm{half}", (None, 0)),
                     self.war.pop("gbuf", (None, 0))]
                E("sy", lambda e, r0=TOK_W + mt0 + a * 128,
                  d=hfx("xtm", half * 1024, 1024):
                  e.dma_start(out=d, in_=self.xe[r0:r0 + 128, :]),
                  waits=w, inc="RX2", inc_k=16)
                rx2 = ("RX2", self.cnt["RX2"])
                for og in range(G_):
                    bank = 2 * half + og // 4
                    w0 = []
                    if og == 0:
                        w0 = [t_ms, self.war.pop(f"ps{2 * half}", (None, 0)),
                              self.war.pop(f"ps{2 * half + 1}", (None, 0))]
                    E("pe", lambda e, i=self.xap(og, 128, a * 128),
                      o=ps[bank][:, (og % 4) * 128:(og % 4) * 128 + 128]:
                      e.transpose(o, i, self.hfx("ident", 0, 128)),
                      waits=w0, inc="PE" if og == G_ - 1 else None)
                pev = self.ms("pe")
                for og in range(G_):
                    bank = 2 * half + og // 4
                    w0 = [pev, rx2] if og == 0 else []
                    if og == 0:
                        w0.append(self.war.pop(f"outtm{half}", (None, 0)))
                    E("dv", lambda e, p_=ps[bank][:, (og % 4) * 128:(og % 4) * 128 + 128],
                      x_=hfx("xtm", half * 1024 + og * 128, 128),
                      o=hfx("p0in", half * 1024 + og * 128, 128):
                      e.tensor_tensor(out=o, in0=x_, in1=p_, op=ALU.add),
                      waits=w0, inc="DV" if og == G_ - 1 else None)
                dvv = self.ms("dv")
                self.war[f"ps{2 * half}"] = dvv
                self.war[f"ps{2 * half + 1}"] = dvv
                self.war[f"xtm_pc{half}"] = dvv
                E("sy", lambda e, o=self.y[mt0 + a * 128: mt0 + a * 128 + 128, :],
                  i=hfx("p0in", half * 1024, 1024):
                  e.dma_start(out=o, in_=i), waits=[dvv], inc="SY", inc_k=16)
                self.war[f"outtm{half}"] = ("SY", self.cnt["SY"])
            self.war["xa_pc"] = self.ms("dv")
            self.war["s_pc"] = self.ms("dv")
        self.war["acc"] = self.ms("ac")
        self.war["xn_pc"] = self.ms("pe")

    def finalize_streams(self, extra_final_waits=()):
        for s, v in extra_final_waits:
            self.stream["sy"].append(lambda e, s=s, v=v: e.wait_ge(self.S[s], v))

        blk = ExitStack()
        block = blk.enter_context(self.nc.Block())
        for eng_name, attr in [("sy", "sync"), ("pe", "tensor"), ("dv", "vector"),
                               ("ac", "scalar"), ("pl", "gpsimd")]:
            items = self.stream[eng_name]

            def make(items):
                def body(e):
                    for f in items:
                        f(e)
                return body
            getattr(block, attr)(make(items))
        blk.close()


def build_full(pc=True):
    mk = MK()
    ctx = ExitStack()
    mk.enter(ctx)
    mk.gen_p0()
    mk.gen_startup()
    mk.emit_p0()
    pend = None
    for t in range(NT):
        if pend is not None:
            stA = mk.gen_pb_a(pend)
        bc = mk.gen_pa_head(t)
        if pend is not None:
            mk.gen_pb_b(pend, stA)
        pend = mk.gen_pa_tail(t, bc)
    if pend is not None:
        stA = mk.gen_pb_a(pend)
        mk.gen_pb_b(pend, stA)
    finals = [("SXN", mk.cnt["SXN"]), ("SS", mk.cnt["SS"])]
    if pc:
        for u in range(PC_STS):
            mk.gen_pc_st(u)
        if PC_STS < 4 or PC_PARTS < 3:
            mk.E("dv", lambda e: e.memset(mk.hfx("xtm", 0, 1024), 0.0), inc="DV")
            mk.drain("dv")
            dv = mk.ms("dv")
            for r0 in range(0, TOK_M, 128):
                mk.E("sy", lambda e, r=r0: e.dma_start(out=mk.y[r:r + 128, :],
                                                       in_=mk.hfx("xtm", 0, 1024)),
                     waits=[dv] if r0 == 0 else [], inc="SY", inc_k=16)
        finals.append(("SY", mk.cnt["SY"]))
    else:
        mk.E("dv", lambda e: e.memset(mk.hfx("xtm", 0, 1024), 0.0), inc="DV")
        mk.drain("dv")
        dv = mk.ms("dv")
        for r0 in range(0, TOK_M, 128):
            mk.E("sy", lambda e, r=r0: e.dma_start(out=mk.y[r:r + 128, :],
                                                   in_=mk.hfx("xtm", 0, 1024)),
                 waits=[dv] if r0 == 0 else [], inc="SY", inc_k=16)
        finals.append(("SY", mk.cnt["SY"]))
    mk.finalize_streams(finals)
    return mk, ctx


def kernel(**inputs):
    in_maps = host_prep(inputs)
    mk, ctx = build_full(pc=True)
    try:
        res = run_bass_kernel_spmd(mk.nc, in_maps, core_ids=list(range(NCORES)))
    finally:
        ctx.close()
    out = np.empty((2, 16384, D), np.float32)
    for c in range(NCORES):
        b, s = c // 4, c % 4
        out[b, s * TOK_M:(s + 1) * TOK_M] = res.results[c]["y"]
    return out
